# version 20
# baseline (speedup 1.0000x reference)
import contextlib
import numpy as np
import ml_dtypes
import concourse.bass as bass
import concourse.mybir as mybir
from concourse.bass_utils import run_bass_kernel_spmd

F32 = mybir.dt.float32
BF16 = mybir.dt.bfloat16
AF = mybir.ActivationFunctionType
ALU = mybir.AluOpType
AX = mybir.AxisListType
BF = ml_dtypes.bfloat16

T = 4096
C = 1024
D = 64
DH = 4096
NCC = C // 128
NHC = DH // 128
NT = T // 128
NQ = T // 512
NPAIR = 8
EPS = 1e-5
EXP_SHIFT = -2.0

ENGINES = ("sync", "gpsimd", "scalar", "vector", "tensor")
BUILD_PHASES = "ABDE"
DEBUG_TAPS = ()


class Sem:
    __slots__ = ("h", "count", "only_engine", "is_dma", "name")

    def __init__(self, nc, name, is_dma=False):
        self.h = nc.alloc_semaphore(name)
        self.name = name
        self.count = 0
        self.only_engine = None
        self.is_dma = is_dma


class Prog:
    def __init__(self, nc):
        self.nc = nc
        self.items = {e: [] for e in ENGINES}
        self.barrier = ()

    def emit(self, eng, fn, waits=(), incs=(), fwaits=()):
        ws = []
        for s, v in tuple(fwaits):
            if v <= 0:
                continue
            assert v <= s.count, f"wait beyond emitted incs on {s.name}: {v} > {s.count}"
            ws.append((s, v))
        for s, v in tuple(waits) + tuple(self.barrier):
            if v <= 0:
                continue
            assert v <= s.count, f"wait beyond emitted incs on {s.name}: {v} > {s.count}"
            if (not s.is_dma) and s.only_engine == eng:
                continue
            ws.append((s, v))
        for s, n in incs:
            s.count += n
            if s.only_engine is None:
                s.only_engine = eng
            elif s.only_engine != eng:
                s.only_engine = False
        self.items[eng].append((tuple(ws), fn, tuple(incs)))

    def dma(self, q, out, in_, waits=(), sem=None, **kw):
        s = sem if sem is not None else self.dma_sems[q]
        self.emit(q, lambda e, o=out, i=in_, k=kw: e.dma_start(out=o, in_=i, **k),
                  waits, ((s, 16),))
        return s.count

    def set_barrier(self, full=True):
        sems = self.all_sems if full else self.compute_sems
        self.barrier = tuple((s, s.count) for s in sems if s.count > 0)

    def finalize(self, block):
        for eng in ENGINES:
            items = self.items[eng]

            def mk(eng_items):
                def run(e):
                    waited = {}
                    for ws, fn, incs in eng_items:
                        for s, v in ws:
                            if waited.get(id(s), -1) >= v:
                                continue
                            e.wait_ge(s.h, v)
                            waited[id(s)] = v
                        if fn is None:
                            continue
                        ins = fn(e)
                        for s, n in incs:
                            ins = ins.then_inc(s.h, n)
                return run

            getattr(block, eng)(mk(items))


def ts(i, n):
    return slice(i * n, (i + 1) * n)


def build_program():
    nc = bass.Bass("TRN2", target_bir_lowering=False)

    x_in_d = nc.dram_tensor("x", [T, C], F32, kind="ExternalInput").ap()
    wq_d = nc.dram_tensor("wq", [C, C], F32, kind="ExternalInput").ap()
    wk_d = nc.dram_tensor("wk", [C, C], F32, kind="ExternalInput").ap()
    wv_d = nc.dram_tensor("wv", [C, C], F32, kind="ExternalInput").ap()
    wp_d = nc.dram_tensor("wp", [C, C], F32, kind="ExternalInput").ap()
    bq_d = nc.dram_tensor("bq2", [128, NCC], F32, kind="ExternalInput").ap()
    bk_d = nc.dram_tensor("bk2", [128, NCC], F32, kind="ExternalInput").ap()
    bv_d = nc.dram_tensor("bv2", [1, C], F32, kind="ExternalInput").ap()
    w1_d = nc.dram_tensor("w1", [DH, C], F32, kind="ExternalInput").ap()
    w2_d = nc.dram_tensor("w2", [DH, C], F32, kind="ExternalInput").ap()
    h1_d = nc.dram_tensor("h1", [DH, DH], F32, kind="ExternalInput").ap()
    h2_d = nc.dram_tensor("h2", [DH, DH], F32, kind="ExternalInput").ap()
    w3_d = nc.dram_tensor("w3", [C, DH], F32, kind="ExternalInput").ap()
    w1s_d = nc.dram_tensor("w1s2", [128, NHC], F32, kind="ExternalInput").ap()
    w2s_d = nc.dram_tensor("w2s2", [128, NHC], F32, kind="ExternalInput").ap()
    h1s_d = nc.dram_tensor("h1s2", [128, NHC], F32, kind="ExternalInput").ap()
    h2s_d = nc.dram_tensor("h2s2", [128, NHC], F32, kind="ExternalInput").ap()
    w3s_d = nc.dram_tensor("w3s2", [128, NCC], F32, kind="ExternalInput").ap()
    ident_d = nc.dram_tensor("cst_ident", [128, 128], BF16, kind="ExternalInput").ap()
    swap_d = nc.dram_tensor("cst_swap", [128, 128], BF16, kind="ExternalInput").ap()
    ones_d = nc.dram_tensor("cst_ones", [1, 128], BF16, kind="ExternalInput").ap()
    cos_d = nc.dram_tensor("cst_cos", [128, T], BF16, kind="ExternalInput").ap()
    sin_d = nc.dram_tensor("cst_sin", [128, T], BF16, kind="ExternalInput").ap()
    mask_d = nc.dram_tensor("cst_mask", [128, 5 * 512], BF16, kind="ExternalInput").ap()

    out_d = nc.dram_tensor("out", [T, C], F32, kind="ExternalOutput").ap()

    def scratch(name, shape, dt):
        kind = "ExternalOutput" if name in DEBUG_TAPS else "Internal"
        return nc.dram_tensor(name, shape, dt, kind=kind).ap()

    w1tt_d = nc.dram_tensor("w1tt", [NHC, 128, NCC, 128], BF16, kind="Internal").ap()
    w2tt_d = nc.dram_tensor("w2tt", [NHC, 128, NCC, 128], BF16, kind="Internal").ap()
    h1tt_d = nc.dram_tensor("h1tt", [NHC, 128, NHC, 128], BF16, kind="Internal").ap()
    h2tt_d = nc.dram_tensor("h2tt", [NHC, 128, NHC, 128], BF16, kind="Internal").ap()
    w3tt_d = nc.dram_tensor("w3tt", [NCC, 128, NHC, 128], BF16, kind="Internal").ap()
    attn_sp_d = scratch("attn_sp", [NPAIR, 128, T], BF16)
    xn2_sp_d = scratch("xn2_sp", [NCC, 128, T], BF16)
    x1_sp_d = scratch("x1_sp", [T, C], F32)

    A = nc.alloc_sbuf_tensor
    ident = A("ident", [128, 128], BF16)
    ones_r = A("ones_r", [1, 128], BF16)
    bq_s = A("bq_s", [128, NCC], F32)
    bk_s = A("bk_s", [128, NCC], F32)
    bv_r = A("bv_r", [1, C], F32)
    bv_rb = A("bv_rb", [1, C], BF16)
    w1s = A("w1s", [128, NHC], F32)
    w2s = A("w2s", [128, NHC], F32)
    h1s = A("h1s", [128, NHC], F32)
    h2s = A("h2s", [128, NHC], F32)
    w3s = A("w3s", [128, NCC], F32)
    eps_t = A("eps_t", [128, 1], F32)
    m2_t = A("m2_t", [128, 1], F32)
    stat = A("stat", [128, 32], F32)
    sq_scr = A("sq_scr", [128, C], BF16)

    P = nc.alloc_psum_tensor
    TRp = P("TRp", [128, 1024], BF16)
    PS1 = P("PS1", [128, 1024], F32)
    PS2 = P("PS2", [128, 1024], F32)
    PA = P("PA", [128, 1024], F32)
    SP = P("SP", [128, 512], F32)
    PBANKS = (PS1[:, 0:512], PS1[:, 512:1024], PS2[:, 0:512], PS2[:, 512:1024],
              PA[:, 0:512], PA[:, 512:1024])

    p = Prog(nc)
    sd_sy = Sem(nc, "sd_sy", True)
    sd_sc = Sem(nc, "sd_sc", True)
    sd_gp = Sem(nc, "sd_gp", True)
    s_te = Sem(nc, "s_te")
    s_ve = Sem(nc, "s_ve")
    s_se = Sem(nc, "s_se")
    s_ge = Sem(nc, "s_ge")
    SX = Sem(nc, "SX", True)
    SQK = Sem(nc, "SQK", True)
    SSP = Sem(nc, "SSP", True)
    SLB = Sem(nc, "SLB", True)
    SX1 = Sem(nc, "SX1", True)
    SXL = Sem(nc, "SXL", True)
    SOUT = Sem(nc, "SOUT", True)
    SXN = Sem(nc, "SXN", True)
    SX2 = Sem(nc, "SX2", True)
    SSL = (Sem(nc, "SSLA", True), Sem(nc, "SSLB", True))
    SPL = tuple(Sem(nc, f"SPL{i}", True) for i in range(3))
    SPS = tuple(Sem(nc, f"SPS{i}", True) for i in range(3))
    SP2 = Sem(nc, "SP2", True)
    p.dma_sems = {"sync": sd_sy, "scalar": sd_sc, "gpsimd": sd_gp}
    p.all_sems = (sd_sy, sd_sc, sd_gp, s_te, s_ve, s_se, s_ge, SX, SQK, SSP, SLB,
                  SX1, SXL, SOUT, SXN, SX2, SSL[0], SSL[1], SP2) + SPL + SPS
    p.compute_sems = (s_te, s_ve, s_se, s_ge)

    es_ab = contextlib.ExitStack()
    es_d = contextlib.ExitStack()
    es_e = contextlib.ExitStack()

    def sb(es, name, shape, dt):
        return es.enter_context(nc.sbuf_tensor(name, shape, dt))

    swap = sb(es_ab, "swap", [128, 128], BF16)
    cosf = sb(es_ab, "cosf", [128, T], BF16)
    sinf = sb(es_ab, "sinf", [128, T], BF16)
    masks = sb(es_ab, "masks", [128, 5 * 512], BF16)
    bv_bc = sb(es_ab, "bv_bc", [128, C], BF16)
    x_ld = sb(es_ab, "x_ld", [128, 2, C], F32)
    xn_tok = sb(es_ab, "xn_tok", [128, 2, C], BF16)
    xn1 = sb(es_ab, "xn1", [128, NCC, T], BF16)
    wq_sl = sb(es_ab, "wq_sl", [128, 2, NCC, 128], BF16)
    wk_sl = sb(es_ab, "wk_sl", [128, 2, NCC, 128], BF16)
    wv_sl = sb(es_ab, "wv_sl", [128, 2, NCC, 128], BF16)
    q_rot = sb(es_ab, "q_rot", [128, T], BF16)
    k_rot = sb(es_ab, "k_rot", [128, T], BF16)
    qk_raw = sb(es_ab, "qk_raw", [128, 2, 512], BF16)
    tmp1 = sb(es_ab, "tmp1", [128, 512], BF16)
    tmp2 = sb(es_ab, "tmp2", [128, 512], BF16)
    v_pair = sb(es_ab, "v_pair", [128, NT, 160], BF16)
    expP = sb(es_ab, "expP", [128, 2, 1024], BF16)
    attn_un = sb(es_ab, "attn_un", [128, 512], BF16)
    recip_b = sb(es_ab, "recip_b", [1, 1024], BF16)
    recip_f = sb(es_ab, "recip_f", [1, 1024], F32)
    attn_nrm = sb(es_ab, "attn_nrm", [128, T], BF16)
    qkv_f32 = sb(es_ab, "qkv_f32", [128, NCC, 128], F32)
    pp_f32 = sb(es_ab, "pp_f32", [128, 3, 1024], F32)
    pp_bf = sb(es_ab, "pp_bf", [128, 3, 1024], BF16)
    pp_st = sb(es_ab, "pp_st", [128, 2, 1024], BF16)

    for dst, src in ((ident, ident_d), (swap, swap_d), (ones_r, ones_d),
                     (cosf, cos_d), (sinf, sin_d), (masks, mask_d),
                     (bq_s, bq_d), (bk_s, bk_d), (bv_r, bv_d),
                     (w1s, w1s_d), (w2s, w2s_d), (h1s, h1s_d), (h2s, h2s_d),
                     (w3s, w3s_d)):
        v_pre = p.dma("scalar", dst[:], src)
    p.emit("vector", lambda e: e.memset(eps_t[:], EPS), incs=((s_ve, 1),))
    p.emit("vector", lambda e: e.memset(m2_t[:], EXP_SHIFT), incs=((s_ve, 1),))
    p.emit("vector", lambda e: e.tensor_copy(bv_rb[:], bv_r[:]),
           waits=((sd_sc, v_pre),), incs=((s_ve, 1),))
    v_bvrb = s_ve.count
    p.emit("tensor", lambda e: e.matmul(SP[:, :], ones_r[:, :], bv_rb[:, 0:512],
                                        start=True, stop=True),
           waits=((s_ve, v_bvrb), (sd_sc, v_pre)))
    p.emit("tensor", lambda e: e.matmul(PA[:, 0:512], ones_r[:, :], bv_rb[:, 512:1024],
                                        start=True, stop=True), incs=((s_te, 1),))
    v_bvmm = s_te.count
    p.emit("vector", lambda e: e.tensor_copy(bv_bc[:, 0:512], SP[:, :]),
           waits=((s_te, v_bvmm),))
    p.emit("vector", lambda e: e.tensor_copy(bv_bc[:, 512:1024], PA[:, 0:512]),
           incs=((s_ve, 1),))
    v_bvbc = s_ve.count
    p.emit("vector", lambda e: e.memset(v_pair[:, :, 64:65], 1.0), incs=((s_ve, 1),))
    p.emit("vector", lambda e: e.memset(v_pair[:, :, 144:145], 1.0), incs=((s_ve, 1),))
    v_ones_v = s_ve.count

    qkv_state = {"i": 0, "cast_v": {}}

    def load_qkv_slice(mat_d, dst_sl, j, war_waits=()):
        i = qkv_state["i"]
        qkv_state["i"] = i + 1
        wl = []
        if i >= 1:
            wl.append((s_ve, qkv_state["cast_v"][i - 1]))
        v_ld = p.dma("scalar", qkv_f32[:],
                     mat_d[:, ts(j, 128)].rearrange("(cc p) o -> p cc o", p=128),
                     waits=wl, sem=SQK)
        p.emit("vector",
               lambda e, dst_sl=dst_sl, j=j: e.tensor_copy(
                   dst_sl[:, j % 2, :, :], qkv_f32[:]),
               waits=[(SQK, v_ld)] + list(war_waits), incs=((s_ve, 1),))
        qkv_state["cast_v"][i] = s_ve.count
        return s_ve.count

    v_wq, v_wk, v_wv = {}, {}, {}
    for j in range(2):
        v_wq[j] = load_qkv_slice(wq_d, wq_sl, j)
        v_wk[j] = load_qkv_slice(wk_d, wk_sl, j)
        v_wv[j] = load_qkv_slice(wv_d, wv_sl, j)

    MATS = (("w1", w1_d, w1tt_d, DH, C), ("w2", w2_d, w2tt_d, DH, C),
            ("h1", h1_d, h1tt_d, DH, DH), ("h2", h2_d, h2tt_d, DH, DH),
            ("w3", w3_d, w3tt_d, C, DH))
    pp_units = []
    pp_last = {}
    for name, W_dd, wtt_dd, odim, cdim in MATS:
        for oc in range(odim // 128):
            for h0 in range(0, cdim, 1024):
                pp_units.append((name, W_dd, wtt_dd, oc, h0, min(1024, cdim - h0)))
                pp_last[name] = len(pp_units) - 1
    pp_state = {"ld": 0, "c1": 0, "tr": 0, "ld_v": {}, "cast_v": {}, "tr_v": {},
                "ev_v": {}, "st_v": {}}
    wt_done_v = {}
    trp_guard = [(s_ve, 0)]

    def pp_load_step():
        i = pp_state["ld"]
        if i < len(pp_units) and (i < 3 or (i - 3) in pp_state["cast_v"]):
            name, W_dd, wtt_dd, oc, h0, hw = pp_units[i]
            wl = []
            if i >= 3:
                wl.append((s_ge, pp_state["cast_v"][i - 3]))
            pp_state["ld_v"][i] = p.dma("sync", pp_f32[:, i % 3, 0:hw],
                                        W_dd[ts(oc, 128), h0:h0 + hw], waits=wl,
                                        sem=SPL[i % 3])
            pp_state["ld"] = i + 1
        c = pp_state["c1"]
        if (c < pp_state["ld"]
                and (c < pp_state["ld"] - 2 or pp_state["ld"] == len(pp_units))
                and (c < 3 or (c - 3) in pp_state["tr_v"])):
            name, W_dd, wtt_dd, oc, h0, hw = pp_units[c]
            d = c % 3
            wl = [(SPL[d], pp_state["ld_v"][c])]
            if c >= 3:
                wl.append((s_te, pp_state["tr_v"][c - 3]))
            p.emit("gpsimd",
                   lambda e, d=d, hw=hw: e.tensor_copy(pp_bf[:, d, 0:hw],
                                                       pp_f32[:, d, 0:hw]),
                   waits=wl, incs=((s_ge, 1),))
            pp_state["cast_v"][c] = s_ge.count
            pp_state["c1"] = c + 1
        return pp_state["c1"] < len(pp_units)

    def pp_tr_step():
        t = pp_state["tr"]
        if t >= min(pp_state["c1"], len(pp_units)):
            return pp_state["tr"] < len(pp_units)
        name, W_dd, wtt_dd, oc, h0, hw = pp_units[t]
        d = t % 3
        e2 = t % 2
        nblk = hw // 128
        for k in range(nblk):
            w2_ = [(s_ge, pp_state["cast_v"][t])]
            if k == 0:
                w2_.append(trp_guard[0])
            p.emit("tensor",
                   lambda e, d=d, k=k: e.transpose(
                       TRp[:, ts(k, 128)], pp_bf[:, d, ts(k, 128)], ident[:]),
                   waits=w2_, incs=((s_te, 1),) if k == nblk - 1 else ())
        pp_state["tr_v"][t] = s_te.count
        wl = [(s_te, pp_state["tr_v"][t])]
        if t >= 2:
            wl.append((SPS[e2], pp_state["st_v"][t - 2]))
        p.emit("vector",
               lambda e, e2=e2, nblk=nblk: e.tensor_copy(
                   pp_st[:, e2, 0:nblk * 128], TRp[:, 0:nblk * 128]),
               waits=wl, incs=((s_ve, 1),))
        trp_guard[0] = (s_ve, s_ve.count)
        b0 = h0 // 128
        pp_state["st_v"][t] = p.dma("sync", wtt_dd[oc, :, b0:b0 + nblk, :],
                                    pp_st[:, e2, 0:hw].rearrange("p (a b) -> p a b", b=128),
                                    waits=[(s_ve, s_ve.count)], sem=SPS[e2])
        if pp_last[name] == t:
            wt_done_v[name] = (SPS[0].count, SPS[1].count)
        pp_state["tr"] = t + 1
        return pp_state["tr"] < len(pp_units)

    def emit_pp_unit():
        return pp_load_step()

    def pp2_step():
        pp_load_step()
        return pp_tr_step()

    ln_sq_v = {}

    def ln(src_ap, dst_ap, sb_base, dep_waits, war_waits):
        su = stat[:, sb_base + 0:sb_base + 1]
        nm = stat[:, sb_base + 1:sb_base + 2]
        sq = stat[:, sb_base + 2:sb_base + 3]
        sd_ = stat[:, sb_base + 3:sb_base + 4]
        rs = stat[:, sb_base + 4:sb_base + 5]
        dep_waits = list(dep_waits)
        p.emit("vector", lambda e: e.tensor_reduce(su, src_ap, AX.X, ALU.add),
               waits=dep_waits, incs=((s_ve, 1),))
        v_su = s_ve.count
        p.emit("vector", lambda e: e.tensor_scalar(nm, su, -1.0 / C, None, ALU.mult),
               waits=[(s_se, ln_sq_v.get(sb_base, 0))], fwaits=[(s_ve, v_su)],
               incs=((s_ve, 1),))
        v_nm = s_ve.count
        p.emit("scalar", lambda e: e.activation(sq_scr[:], src_ap, AF.Square,
                                                bias=nm, accum_out=sq),
               waits=dep_waits + [(s_ve, v_nm)], incs=((s_se, 1),))
        v_sq = s_se.count
        ln_sq_v[sb_base] = v_sq
        p.emit("scalar", lambda e: e.activation(sd_, sq, AF.Sqrt, bias=eps_t[:],
                                                scale=1.0 / C),
               waits=[(s_ve, 2)], fwaits=[(s_se, v_sq)], incs=((s_se, 1),))
        v_sd = s_se.count
        p.emit("vector", lambda e: e.reciprocal(rs, sd_), waits=[(s_se, v_sd)],
               incs=((s_ve, 1),))
        v_rs = s_ve.count
        p.emit("vector", lambda e: e.tensor_scalar(dst_ap, src_ap, nm, rs,
                                                   ALU.add, ALU.mult),
               waits=list(war_waits), fwaits=[(s_ve, v_rs)], incs=((s_ve, 1),))
        return s_ve.count

    xld_v = {}
    xnA_copy_v = {}
    normA_v = {}
    sqA_v = {}
    if "A" in BUILD_PHASES:
        def a_ln(i):
            w = []
            if i >= 2:
                w += [(s_ve, normA_v[i - 2]), (s_se, sqA_v[i - 2])]
            xld_v[i] = p.dma("scalar", x_ld[:, i % 2, :], x_in_d[ts(i, 128), :],
                             waits=w, sem=SX)
            war = [(s_te, trA_v[i - 2])] if i >= 2 else []
            normA_v[i] = ln(x_ld[:, i % 2, :], xn_tok[:, i % 2, :], 8 * (i % 2),
                            [(SX, xld_v[i])], war)
            sqA_v[i] = s_se.count

        def a_tr(i):
            for cc in range(NCC):
                w2 = [(s_ve, normA_v[i])]
                if cc == 0:
                    w2 += [(sd_sc, v_pre), trp_guard[0]]
                p.emit("tensor",
                       lambda e, cc=cc, i=i: e.transpose(TRp[:, ts(cc, 128)],
                                                         xn_tok[:, i % 2, ts(cc, 128)],
                                                         ident[:]),
                       waits=w2, incs=((s_te, 1),) if cc == NCC - 1 else ())
            trA_v[i] = s_te.count
            p.emit("vector",
                   lambda e, i=i: e.tensor_copy(
                       xn1[:, :, ts(i, 128)],
                       TRp[:].rearrange("p (a b) -> p a b", b=128)),
                   waits=[(s_te, trA_v[i])], incs=((s_ve, 1),))
            trp_guard[0] = (s_ve, s_ve.count)
            xnA_copy_v[i] = s_ve.count

        trA_v = {}
        for i in range(NT):
            a_ln(i)
            if i >= 1:
                a_tr(i - 1)
                pp2_step()
        a_tr(NT - 1)
        pp2_step()
        if "xn1" in DEBUG_TAPS:
            tap = nc.dram_tensor("tap_xn1", [128, NCC * T], BF16, kind="ExternalOutput").ap()
            p.dma("scalar", tap, xn1[:].rearrange("p a b -> p (a b)"),
                  waits=((s_ve, s_ve.count),))

    spill_v = {}
    if "B" in BUILD_PHASES:
        b1_mm_ring = {}
        b2_mm_ring = {}
        v_tail = {"cp": 0, "ln": 0, "fm": 0}
        for j in range(NPAIR):
            jr = j % 2
            if j + 1 < NPAIR and j + 1 >= 2:
                jp = j + 1
                v_wq[jp] = load_qkv_slice(wq_d, wq_sl, jp, [(s_te, b1_mm_ring[jp % 2])])
                v_wk[jp] = load_qkv_slice(wk_d, wk_sl, jp, [(s_te, b1_mm_ring[jp % 2])])
                v_wv[jp] = load_qkv_slice(wv_d, wv_sl, jp, [(s_te, b2_mm_ring[jp % 2])])
            vmm_v = {}
            vevac_v = {}

            def b2_mm(kc):
                pv = SP[:, 0:128] if kc % 2 == 0 else PS1[:, 512:640]
                for cc in range(NCC):
                    w0 = []
                    if cc == 0:
                        w0 = [(s_ve, xnA_copy_v[kc]), (s_ve, v_wv[j])]
                        if kc >= 2:
                            w0.append((s_ve, vevac_v[kc - 2]))
                        else:
                            w0 += [(s_ve, s_ve.count), (s_se, s_se.count)]
                    p.emit("tensor",
                           lambda e, kc=kc, cc=cc, jr=jr, pv=pv: e.matmul(
                               pv, xn1[:, cc, ts(kc, 128)], wv_sl[:, jr, cc, :],
                               start=(cc == 0), stop=(cc == NCC - 1)),
                           waits=w0, incs=((s_te, 1),) if cc == NCC - 1 else ())
                vmm_v[kc] = s_te.count

            def b2_evac(kc):
                pv = SP[:, 0:128] if kc % 2 == 0 else PS1[:, 512:640]
                for hh in range(2):
                    p.emit("vector",
                           lambda e, kc=kc, hh=hh, j=j, pv=pv: e.tensor_tensor(
                               v_pair[:, kc, 80 * hh:80 * hh + 64],
                               pv[:, 64 * hh:64 * hh + 64],
                               bv_bc[:, j * 128 + 64 * hh:j * 128 + 64 * hh + 64],
                               ALU.add),
                           waits=[(s_te, vmm_v[kc]), (s_ve, v_bvbc), (s_ve, v_ones_v)],
                           incs=((s_ve, 1),) if hh == 1 else ())
                vevac_v[kc] = s_ve.count

            b2n = [0, 0]

            def b2_tick():
                if b2n[0] < NT:
                    b2_mm(b2n[0])
                    b2n[0] += 1
                    if b2n[1] < b2n[0] - 1:
                        b2_evac(b2n[1])
                        b2n[1] += 1
                elif b2n[1] < NT:
                    b2_evac(b2n[1])
                    b2n[1] += 1

            units = [(tt, qk) for tt in range(NQ) for qk in range(2)]
            evac_v = {}
            swap_v = {}
            rope_v = {}

            PBQ = (PA[:, 0:512], PA[:, 512:1024], PS1[:, 0:512])

            def b1_proj(u):
                tt, qk = units[u]
                wmat = wq_sl if qk == 0 else wk_sl
                v_w = (v_wq if qk == 0 else v_wk)[j]
                w_xn = (s_ve, xnA_copy_v[4 * tt + 3])
                pq = PBQ[u % 3]
                for cc in range(NCC):
                    w0 = []
                    if cc == 0:
                        w0 = [w_xn, (s_ve, v_w)]
                        if u >= 3:
                            w0.append((s_se, evac_v[u - 3]))
                        else:
                            w0 += [(s_se, s_se.count), (s_ve, s_ve.count)]
                    p.emit("tensor",
                           lambda e, cc=cc, qk=qk, wmat=wmat, tt=tt, jr=jr, pq=pq: e.matmul(
                               pq, wmat[:, jr, cc, :],
                               xn1[:, cc, ts(tt, 512)],
                               start=(cc == 0), stop=(cc == NCC - 1)),
                           waits=w0, incs=((s_te, 1),) if cc == NCC - 1 else ())
                v_mm = s_te.count
                bias_s = bq_s if qk == 0 else bk_s
                p.emit("scalar",
                       lambda e, qk=qk, j=j, bias_s=bias_s, pq=pq: e.activation(
                           qk_raw[:, qk, :], pq, AF.Identity,
                           bias=bias_s[:, j:j + 1]),
                       waits=[(s_te, v_mm), (sd_sc, v_pre)] +
                             ([(s_ve, rope_v[u - 2])] if u >= 2 else []),
                       incs=((s_se, 1),))
                evac_v[u] = s_se.count

            def b1_rope(u):
                tt, qk = units[u]
                w0 = [(s_se, evac_v[u]), (sd_sc, v_pre)]
                if u >= 2:
                    w0.append((s_ve, rope_v[u - 2]))
                p.emit("tensor",
                       lambda e, qk=qk: e.matmul(
                           PS2[:, ts(qk, 512)], swap[:], qk_raw[:, qk, :],
                           start=True, stop=True),
                       waits=w0, incs=((s_te, 1),))
                v_sw = s_te.count
                swap_v[u] = v_sw
                p.emit("vector",
                       lambda e, qk=qk, tt=tt: e.scalar_tensor_tensor(
                           tmp2[:], PS2[:, ts(qk, 512)], 1.0, sinf[:, ts(tt, 512)],
                           ALU.mult, ALU.mult),
                       waits=[(s_te, v_sw), (sd_sc, v_pre)])
                p.emit("vector",
                       lambda e, qk=qk, tt=tt: e.tensor_tensor(
                           tmp1[:], qk_raw[:, qk, :], cosf[:, ts(tt, 512)], ALU.mult),
                       waits=[(s_se, evac_v[u])])
                rot = q_rot if qk == 0 else k_rot
                p.emit("vector",
                       lambda e, tt=tt, rot=rot: e.tensor_tensor(
                           rot[:, ts(tt, 512)], tmp1[:], tmp2[:], ALU.add),
                       incs=((s_ve, 1),))
                rope_v[u] = s_ve.count

            for u in range(len(units)):
                b1_proj(u)
                if u >= 1:
                    b1_rope(u - 1)
                b2_tick()
                b2_tick()
                pp2_step()
            b1_rope(len(units) - 1)
            v_rot_done = s_ve.count
            b1_mm_ring[jr] = s_te.count
            while b2n[1] < NT:
                b2_tick()
            v_vdone = s_ve.count
            b2_mm_ring[jr] = vmm_v[NT - 1]
            exp_bank_v = {0: s_se.count, 1: s_se.count}
            for tt in range(NQ):
                kc_hi = min(4 * tt + 4, NT - 1)
                n_kc = kc_hi + 1
                pend = None

                def b3_scores_exp(kk):
                    Pb = kk % 2
                    off = max(0, 128 * kk - 512 * tt - 64)
                    psx = PS1 if Pb == 0 else PS2
                    p.emit("tensor",
                           lambda e, kk=kk, tt=tt, psx=psx, off=off: e.matmul(
                               psx[:, off:512], k_rot[0:64, ts(kk, 128)],
                               q_rot[0:64, tt * 512 + off:(tt + 1) * 512],
                               start=True, stop=True),
                           waits=[(s_ve, v_vdone), (s_se, exp_bank_v[Pb])])
                    p.emit("tensor",
                           lambda e, kk=kk, tt=tt, psx=psx, off=off: e.matmul(
                               psx[:, 512 + off:1024], k_rot[64:128, ts(kk, 128)],
                               q_rot[64:128, tt * 512 + off:(tt + 1) * 512],
                               start=True, stop=True, tile_position=(64, 0)),
                           incs=((s_te, 1),))
                    v_sc = s_te.count
                    p.emit("scalar",
                           lambda e, Pb=Pb, psx=psx, off=off: e.activation(
                               expP[:, Pb, :].rearrange("p (h q) -> p h q", q=512)[:, :, off:],
                               psx[:].rearrange("p (h q) -> p h q", q=512)[:, :, off:],
                               AF.Exp, bias=m2_t[:], scale=0.125),
                           waits=[(s_te, v_sc), (s_ve, 2)], incs=((s_se, 1),))
                    v_exp = s_se.count
                    exp_bank_v[Pb] = v_exp
                    sidx = kk - 4 * tt
                    dep_av = (s_se, v_exp)
                    if sidx >= 0:
                        for hh in range(2):
                            p.emit("vector",
                                   lambda e, Pb=Pb, sidx=sidx, hh=hh, off=off: e.tensor_tensor(
                                       expP[:, Pb, hh * 512 + off:(hh + 1) * 512],
                                       expP[:, Pb, hh * 512 + off:(hh + 1) * 512],
                                       masks[:, sidx * 512 + off:(sidx + 1) * 512], ALU.mult),
                                   waits=[(s_se, v_exp), (sd_sc, v_pre)],
                                   incs=((s_ve, 1),) if hh == 1 else ())
                        dep_av = (s_ve, s_ve.count)
                    return dep_av

                def b3_av(kk, dep_av, last):
                    Pb = kk % 2
                    off = max(0, 128 * kk - 512 * tt - 64)
                    for hh in range(2):
                        w0 = [dep_av, (s_ve, v_vdone)]
                        if kk == 0 and hh == 0:
                            w0 += [(s_ve, v_tail["cp"]), (s_se, v_tail["ln"])]
                        p.emit("tensor",
                               lambda e, kk=kk, hh=hh, Pb=Pb, last=last, off=off: e.matmul(
                                   PA[0:65, hh * 512 + off:(hh + 1) * 512],
                                   v_pair[:, kk, 80 * hh:80 * hh + 65],
                                   expP[:, Pb, hh * 512 + off:(hh + 1) * 512],
                                   start=(kk == 0), stop=last),
                               waits=w0,
                               incs=((s_te, 1),) if (last and hh == 1) else ())

                for kk in range(n_kc):
                    dep_av = b3_scores_exp(kk)
                    if pend is not None:
                        b3_av(pend[0], pend[1], False)
                    pend = (kk, dep_av)
                    if kk % 3 == 1:
                        pp_load_step()
                    if kk % 9 == 2:
                        pp_tr_step()
                b3_av(pend[0], pend[1], True)
                v_av = s_te.count
                p.emit("vector", lambda e: e.tensor_copy(attn_un[0:64, :], PA[0:64, 0:512]),
                       waits=[(s_te, v_av)])
                p.emit("vector", lambda e: e.tensor_copy(attn_un[64:128, :], PA[0:64, 512:1024]),
                       incs=((s_ve, 1),))
                v_tail["cp"] = s_ve.count
                p.emit("scalar", lambda e: e.activation(recip_f[:], PA[64:65, 0:1024],
                                                        AF.Ln),
                       waits=[(s_te, v_av)], incs=((s_se, 1),))
                v_tail["ln"] = s_se.count
                p.emit("scalar", lambda e: e.activation(recip_b[:], recip_f[:],
                                                        AF.Exp, scale=-1.0),
                       fwaits=[(s_se, v_tail["ln"])], incs=((s_se, 1),))
                v_rec = s_se.count
                p.emit("tensor", lambda e: e.matmul(SP[0:64, :], ones_r[:, 0:64],
                                                    recip_b[:, 0:512], start=True, stop=True),
                       waits=[(s_se, v_rec), (s_ve, v_tail["fm"])])
                p.emit("tensor", lambda e: e.matmul(SP[64:128, :], ones_r[:, 0:64],
                                                    recip_b[:, 512:1024], start=True, stop=True,
                                                    tile_position=(0, 64)),
                       incs=((s_te, 1),))
                v_bc = s_te.count
                w_sp = [(SSP, spill_v[j - 1])] if j >= 1 else []
                p.emit("vector",
                       lambda e, tt=tt: e.tensor_tensor(
                           attn_nrm[:, ts(tt, 512)], attn_un[:], SP[:], ALU.mult),
                       waits=[(s_te, v_bc)] + w_sp, incs=((s_ve, 1),))
                v_tail["fm"] = s_ve.count

            spill_v[j] = p.dma("scalar", attn_sp_d[j], attn_nrm[:],
                               waits=[(s_ve, s_ve.count)], sem=SSP)
            if j == 0:
                for nm_, buf_ in (("qrot0", q_rot), ("krot0", k_rot)):
                    if nm_ in DEBUG_TAPS:
                        tp = nc.dram_tensor("tap_" + nm_, [128, T], BF16,
                                            kind="ExternalOutput").ap()
                        p.dma("scalar", tp, buf_[:], waits=((s_ve, v_rot_done),))
                if "vpair0" in DEBUG_TAPS:
                    tp = nc.dram_tensor("tap_vpair0", [128, NT * 160], BF16,
                                        kind="ExternalOutput").ap()
                    p.dma("scalar", tp, v_pair[:].rearrange("p a b -> p (a b)"),
                          waits=((s_ve, v_vdone),))
        while pp2_step():
            pass

    xn2_sp_v = {}
    x1_sp_v = {}
    if "D" in BUILD_PHASES:
        es_ab.close()
        p.set_barrier(full=True)
        wp_s = sb(es_d, "wp_s", [128, NCC, C], BF16)
        attn_fm = sb(es_d, "attn_fm", [128, NCC, T], BF16)
        y_fm = sb(es_d, "y_fm", [128, 2, 512], BF16)
        y_tok = sb(es_d, "y_tok", [128, 2, 4, C], BF16)
        x_ld2 = sb(es_d, "x_ld2", [128, 2, 4, C], F32)
        x1_t = sb(es_d, "x1_t", [128, 2, C], F32)
        lnout = sb(es_d, "lnout", [128, 2, C], F32)
        xn2_t = sb(es_d, "xn2_t", [128, 2, C], BF16)
        xn2_st = sb(es_d, "xn2_st", [128, 2, NCC * 128], BF16)

        wp_f32 = sb(es_d, "wp_f32", [128, NCC, 512], F32)
        v_wp_cast = 0
        for hf in range(2):
            v_ld = p.dma("scalar", wp_f32[:],
                         wp_d[:, ts(hf, 512)].rearrange("(cc p) o -> p cc o", p=128),
                         waits=[(s_ve, v_wp_cast)] if hf else [])
            p.emit("vector",
                   lambda e, hf=hf: e.tensor_copy(
                       wp_s[:].rearrange("p cc (h o) -> p cc h o", h=2)[:, :, hf, :],
                       wp_f32[:]),
                   waits=[(sd_sc, v_ld)], incs=((s_ve, 1),))
            v_wp_cast = s_ve.count
        v_wp = v_wp_cast
        tap_y = None
        if "ytok" in DEBUG_TAPS:
            tap_y = nc.dram_tensor("tap_ytok", [T, C], BF16, kind="ExternalOutput").ap()
        slab_v = {}
        for cc in range(NCC):
            slab_v[cc] = p.dma("scalar", attn_fm[:, cc, :], attn_sp_d[cc],
                               waits=[(SSP, spill_v[cc])], sem=SLB)
        xld2_v = {}
        ycpy_all = {}
        addD_v = {}

        def d_loads(tt):
            for tcb in range(4):
                i = tt * 4 + tcb
                w = []
                if tt >= 2:
                    w.append((s_ve, addD_v[(tt - 2) * 4 + 3]))
                xld2_v[i] = p.dma("scalar", x_ld2[:, tt % 2, tcb, :],
                                  x_in_d[ts(i, 128), :], waits=w, sem=SX)

        def d_mms(tt):
            ymm_v = {}
            yev_v = {}
            ycpy_v = {}

            def d_mm(oc):
                pb = oc % 2
                for cc in range(NCC):
                    w0 = []
                    if cc == 0:
                        w0 = [(SLB, slab_v[NCC - 1]), (s_ve, v_wp)]
                        if oc >= 2:
                            w0.append((s_se, yev_v[oc - 2]))
                        else:
                            w0.append((s_se, s_se.count))
                    p.emit("tensor",
                           lambda e, oc=oc, cc=cc, tt=tt, pb=pb: e.matmul(
                               PS1[:, ts(pb, 512)], wp_s[:, cc, ts(oc, 128)],
                               attn_fm[:, cc, ts(tt, 512)],
                               start=(cc == 0), stop=(cc == NCC - 1)),
                           waits=w0, incs=((s_te, 1),) if cc == NCC - 1 else ())
                ymm_v[oc] = s_te.count
                p.emit("scalar",
                       lambda e, pb=pb: e.activation(y_fm[:, pb, :], PS1[:, ts(pb, 512)],
                                                     AF.Identity),
                       waits=[(s_te, ymm_v[oc])] +
                             ([(s_te, s_te.count)] if oc >= 2 else []),
                       incs=((s_se, 1),))
                yev_v[oc] = s_se.count

            def d_tr(oc):
                pb = oc % 2
                for tcb in range(4):
                    w2 = [(s_se, yev_v[oc])]
                    if tcb == 0:
                        w2 += [trp_guard[0], (s_se, s_se.count)]
                    p.emit("tensor",
                           lambda e, tcb=tcb, pb=pb: e.transpose(
                               TRp[:, ts(tcb, 128)], y_fm[:, pb, ts(tcb, 128)], ident[:]),
                           waits=w2, incs=((s_te, 1),) if tcb == 3 else ())
                v_tr = s_te.count
                p.emit("vector",
                       lambda e, oc=oc, tt=tt: e.tensor_copy(
                           y_tok[:, tt % 2, :, ts(oc, 128)],
                           TRp[:, 0:512].rearrange("p (a b) -> p a b", b=128)),
                       waits=[(s_te, v_tr), (s_se, s_se.count)], incs=((s_ve, 1),))
                trp_guard[0] = (s_ve, s_ve.count)
                ycpy_v[oc] = s_ve.count

            for oc in range(NCC):
                d_mm(oc)
                if oc >= 1:
                    d_tr(oc - 1)
                pp2_step()
            d_tr(NCC - 1)
            ycpy_all[tt] = ycpy_v[NCC - 1]
            if tap_y is not None:
                p.dma("scalar", tap_y.rearrange("(a p) c -> p a c", p=128)[:, tt * 4:tt * 4 + 4, :],
                      y_tok[:, tt % 2, :, :], waits=((s_ve, ycpy_v[NCC - 1]),))

        def d_chain(tt):
            for tcb in range(4):
                i = tt * 4 + tcb
                tb = tcb % 2
                ln(y_tok[:, tt % 2, tcb, :], lnout[:, tb, :], 8 * tb,
                   [(s_ve, ycpy_all[tt])], [(s_ve, s_ve.count)])
                w_x1 = [(SX, xld2_v[tt * 4 + 3]), (s_se, s_se.count)]
                if i >= 2:
                    w_x1.append((SX1, x1_sp_v[i - 2]))
                p.emit("vector",
                       lambda e, tcb=tcb, tb=tb, tt=tt: e.tensor_tensor(
                           x1_t[:, tb, :], lnout[:, tb, :], x_ld2[:, tt % 2, tcb, :],
                           ALU.add),
                       waits=w_x1, incs=((s_ve, 1),))
                v_x1 = s_ve.count
                addD_v[i] = v_x1
                x1_sp_v[i] = p.dma("scalar", x1_sp_d[ts(i, 128), :], x1_t[:, tb, :],
                                   waits=[(s_ve, v_x1)], sem=SX1)
                ln(x1_t[:, tb, :], xn2_t[:, tb, :], 16 + 8 * tb, [(s_ve, v_x1)],
                   [(s_ve, s_ve.count), (s_te, s_te.count)])
                v_n3 = s_ve.count
                for cc in range(NCC):
                    w2 = [(s_ve, v_n3)]
                    if cc == 0:
                        w2.append(trp_guard[0])
                    p.emit("tensor",
                           lambda e, cc=cc, tb=tb: e.transpose(TRp[:, ts(cc, 128)],
                                                               xn2_t[:, tb, ts(cc, 128)],
                                                               ident[:]),
                           waits=w2, incs=((s_te, 1),) if cc == NCC - 1 else ())
                v_tr3 = s_te.count
                w3_ = [(s_te, v_tr3)]
                if i >= 2:
                    w3_.append((SXN, xn2_sp_v[i - 2]))
                p.emit("vector",
                       lambda e, tb=tb: e.tensor_copy(
                           xn2_st[:, tb, :].rearrange("p (a b) -> p a b", b=128),
                           TRp[:].rearrange("p (a b) -> p a b", b=128)),
                       waits=w3_, incs=((s_ve, 1),))
                trp_guard[0] = (s_ve, s_ve.count)
                v_st = s_ve.count
                xn2_sp_v[i] = p.dma("sync",
                                    xn2_sp_d.rearrange("cc c t -> c cc t")[:, :, ts(i, 128)],
                                    xn2_st[:, tb, :].rearrange("p (a b) -> p a b", b=128),
                                    waits=[(s_ve, v_st)], sem=SXN)
                pp2_step()

        for tt in range(NQ):
            d_loads(tt)
            d_mms(tt)
            if tt >= 1:
                d_chain(tt - 1)
        d_chain(NQ - 1)
        while pp2_step() or pp_state["tr"] < len(pp_units):
            pass

    if "E" in BUILD_PHASES:
        es_d.close()
        p.set_barrier(full=True)
        x2sl = sb(es_e, "x2sl", [128, NCC, 512], BF16)
        wslab_t = sb(es_e, "wslab", [128, 2 * NHC * 512], BF16)
        wslab = wslab_t[:].rearrange("p (d oc cc o) -> p d oc cc o", d=2, oc=4, cc=NHC, o=128)
        hbuf = sb(es_e, "hbuf", [128, NHC, 512], BF16)
        g1buf = sb(es_e, "g1buf", [128, NHC, 512], BF16)
        tsil = sb(es_e, "tsil", [128, 2, 512], BF16)
        y2fm = sb(es_e, "y2fm", [128, NCC, 512], BF16)
        y2tok = sb(es_e, "y2tok", [128, C], BF16)
        x1_ld = sb(es_e, "x1_ld", [128, 2, C], F32)
        out_t = sb(es_e, "out_t", [128, 2, C], F32)
        lnoutE = sb(es_e, "lnoutE", [128, C], F32)

        pk = [0]
        bank_free_v = {}

        def next_bank():
            b = pk[0] % 6
            pk[0] += 1
            return b

        out_v = {}
        x2_war_v = 0
        addE_v = {}
        for ti in range(NQ):
            w = [(SXN, max(xn2_sp_v[ti * 4 + k] for k in range(4)))]
            if ti >= 1:
                w.append((s_te, x2_war_v))
            v_x2 = p.dma("sync", x2sl[:],
                         xn2_sp_d.rearrange("cc c t -> c cc t")[:, :, ts(ti, 512)],
                         waits=w, sem=SX2)
            st1_mm_v = {}
            for ob in range(8):
                d = ob % 2
                w1w = [(SPS[0], wt_done_v["w1"][0]), (SPS[1], wt_done_v["w1"][1])]
                if ob >= 2:
                    w1w.append((s_te, st1_mm_v[ob - 2]))
                elif ti >= 1:
                    w1w.append((s_te, x2_war_v))
                slv1 = p.dma("sync", wslab[:, d, :, 0:NCC, :],
                             w1tt_d[ts(ob, 4)].rearrange("oc u cc o -> u oc cc o"),
                             waits=w1w, sem=SSL[d])
                slv2 = p.dma("sync", wslab[:, d, :, NCC:2 * NCC, :],
                             w2tt_d[ts(ob, 4)].rearrange("oc u cc o -> u oc cc o"),
                             waits=[(SPS[0], wt_done_v["w2"][0]),
                                    (SPS[1], wt_done_v["w2"][1])], sem=SSL[d])
                for oc in range(4):
                    hc = ob * 4 + oc
                    b1 = next_bank()
                    b2 = next_bank()
                    for which, bb, slv in ((0, b1, slv1), (1, b2, slv2)):
                        for cc in range(NCC):
                            w0 = []
                            if cc == 0:
                                w0 = [(SSL[d], slv), (SX2, v_x2)]
                                if bb in bank_free_v:
                                    w0.append(bank_free_v[bb])
                            p.emit("tensor",
                                   lambda e, which=which, bb=bb, cc=cc, oc=oc, d=d: e.matmul(
                                       PBANKS[bb], wslab[:, d, oc, which * NCC + cc, :],
                                       x2sl[:, cc, :],
                                       start=(cc == 0), stop=(cc == NCC - 1)),
                                   waits=w0, incs=((s_te, 1),) if cc == NCC - 1 else ())
                    v_mm = s_te.count
                    p.emit("scalar",
                           lambda e, hc=hc, b1=b1: e.activation(
                               tsil[:, hc % 2, :], PBANKS[b1], AF.Silu,
                               scale=w1s[:, hc:hc + 1]),
                           waits=[(s_te, v_mm), (s_ve, s_ve.count), (sd_sc, v_pre)],
                           incs=((s_se, 1),))
                    v_sil = s_se.count
                    bank_free_v[b1] = (s_se, v_sil)
                    p.emit("vector",
                           lambda e, hc=hc, b2=b2: e.scalar_tensor_tensor(
                               hbuf[:, hc, :], PBANKS[b2], w2s[:, hc:hc + 1],
                               tsil[:, hc % 2, :], ALU.mult, ALU.mult),
                           waits=[(s_se, v_sil)], incs=((s_ve, 1),))
                    bank_free_v[b2] = (s_ve, s_ve.count)
                st1_mm_v[ob] = s_te.count
            v_h_done = (s_ve, s_ve.count)
            if "hbuf" in DEBUG_TAPS and ti == 0:
                th = nc.dram_tensor("tap_hbuf", [128, NHC * 512], BF16,
                                    kind="ExternalOutput").ap()
                p.dma("scalar", th, hbuf[:].rearrange("p a b -> p (a b)"), waits=(v_h_done,))
            for wt_name, wt_dd, src, dst, scl in (("h1", h1tt_d, hbuf, g1buf, h1s),
                                                  ("h2", h2tt_d, g1buf, hbuf, h2s)):
                slab_mm_v = {}
                for ob in range(8):
                    d = ob % 2
                    w1w = [(SPS[0], wt_done_v[wt_name][0]),
                           (SPS[1], wt_done_v[wt_name][1])]
                    if ob >= 2:
                        w1w.append((s_te, slab_mm_v[ob - 2]))
                    else:
                        w1w.append((s_te, st1_mm_v[6 + ob]))
                    slv = p.dma("sync", wslab[:, d, :, :, :],
                                wt_dd[ts(ob, 4)].rearrange("oc u cc o -> u oc cc o"),
                                waits=w1w, sem=SSL[d])
                    for oc in range(4):
                        hc = ob * 4 + oc
                        bb = next_bank()
                        for cc in range(NHC):
                            w0 = []
                            if cc == 0:
                                w0 = [(SSL[d], slv), v_h_done]
                                if bb in bank_free_v:
                                    w0.append(bank_free_v[bb])
                            p.emit("tensor",
                                   lambda e, bb=bb, cc=cc, oc=oc, d=d, src=src: e.matmul(
                                       PBANKS[bb], wslab[:, d, oc, cc, :],
                                       src[:, cc, :],
                                       start=(cc == 0), stop=(cc == NHC - 1)),
                                   waits=w0, incs=((s_te, 1),) if cc == NHC - 1 else ())
                        v_mm = s_te.count
                        p.emit("scalar",
                               lambda e, hc=hc, bb=bb, dst=dst, scl=scl: e.activation(
                                   dst[:, hc, :], PBANKS[bb], AF.Gelu,
                                   scale=scl[:, hc:hc + 1]),
                               waits=[(s_te, v_mm)], incs=((s_se, 1),))
                        bank_free_v[bb] = (s_se, s_se.count)
                    slab_mm_v[ob] = s_te.count
                v_h_done = (s_se, s_se.count)
                if "g1buf" in DEBUG_TAPS and ti == 0 and wt_name == "h1":
                    tg = nc.dram_tensor("tap_g1buf", [128, NHC * 512], BF16,
                                        kind="ExternalOutput").ap()
                    p.dma("scalar", tg, g1buf[:].rearrange("p a b -> p (a b)"), waits=(v_h_done,))
                st1_mm_v = {6: slab_mm_v[6], 7: slab_mm_v[7]}
            v_g2_done = v_h_done
            for ob2 in range(2):
                d = ob2 % 2
                slv = p.dma("sync", wslab[:, d, :, :, :],
                            w3tt_d[ts(ob2, 4)].rearrange("oc u cc o -> u oc cc o"),
                            waits=[(SPS[0], wt_done_v["w3"][0]),
                                   (SPS[1], wt_done_v["w3"][1]),
                                   (s_te, st1_mm_v[6 + ob2])], sem=SSL[d])
                for oc in range(4):
                    occ = ob2 * 4 + oc
                    bb = next_bank()
                    for cc in range(NHC):
                        w0 = []
                        if cc == 0:
                            w0 = [(SSL[d], slv), v_g2_done]
                            if bb in bank_free_v:
                                w0.append(bank_free_v[bb])
                        p.emit("tensor",
                               lambda e, bb=bb, cc=cc, oc=oc, d=d: e.matmul(
                                   PBANKS[bb], wslab[:, d, oc, cc, :],
                                   hbuf[:, cc, :],
                                   start=(cc == 0), stop=(cc == NHC - 1)),
                               waits=w0, incs=((s_te, 1),) if cc == NHC - 1 else ())
                    v_mm = s_te.count
                    p.emit("scalar",
                           lambda e, occ=occ, bb=bb: e.activation(
                               y2fm[:, occ, :], PBANKS[bb], AF.Identity,
                               scale=w3s[:, occ:occ + 1]),
                           waits=[(s_te, v_mm), (s_te, s_te.count)], incs=((s_se, 1),))
                    bank_free_v[bb] = (s_se, s_se.count)
            v_y2 = s_se.count
            for tcb in range(4):
                i = ti * 4 + tcb
                tb = tcb % 2
                w_x1 = [(SX1, x1_sp_v[i])]
                if i >= 2:
                    w_x1.append((s_ve, addE_v[i - 2]))
                v_x1ld = p.dma("scalar", x1_ld[:, tb, :], x1_sp_d[ts(i, 128), :],
                               waits=w_x1, sem=SXL)
                for cc in range(NCC):
                    w2 = [(s_se, v_y2)]
                    if cc == 0:
                        w2.append(trp_guard[0])
                    p.emit("tensor",
                           lambda e, cc=cc, tcb=tcb: e.transpose(
                               TRp[:, ts(cc, 128)], y2fm[:, cc, ts(tcb, 128)], ident[:]),
                           waits=w2, incs=((s_te, 1),) if cc == NCC - 1 else ())
                v_tr = s_te.count
                p.emit("vector", lambda e: e.tensor_copy(y2tok[:], TRp[:]),
                       waits=[(s_te, v_tr), (s_se, s_se.count)], incs=((s_ve, 1),))
                trp_guard[0] = (s_ve, s_ve.count)
                ln(y2tok[:], lnoutE[:], 0, [(s_ve, s_ve.count)], [(s_ve, s_ve.count)])
                w_add = [(SXL, v_x1ld)]
                if i >= 2:
                    w_add.append((SOUT, out_v[i - 2]))
                p.emit("vector",
                       lambda e, tb=tb: e.tensor_tensor(
                           out_t[:, tb, :], lnoutE[:], x1_ld[:, tb, :], ALU.add),
                       waits=w_add, incs=((s_ve, 1),))
                addE_v[i] = s_ve.count
                out_v[i] = p.dma("scalar", out_d[ts(i, 128), :], out_t[:, tb, :],
                                 waits=[(s_ve, addE_v[i])], sem=SOUT)
            x2_war_v = s_te.count
        p.emit("sync", None, waits=[(SOUT, out_v[NT - 1])])
        es_e.close()
    else:
        es_ab.close()

    with nc.Block() as block:
        p.finalize(block)
    return nc


def make_constants():
    ident = np.eye(128, dtype=BF)
    swap = np.zeros((128, 128), dtype=np.float32)
    for i in range(64):
        swap[2 * i, 2 * i + 1] = 1.0
        swap[2 * i + 1, 2 * i] = -1.0
    ones = np.ones((1, 128), dtype=BF)
    inv = 1.0 / (10000.0 ** (np.arange(0, D, 2, dtype=np.float32) / D))
    ang = np.arange(T, dtype=np.float32)[:, None] * inv[None, :]
    cos = np.cos(ang).astype(np.float32)
    sin = np.sin(ang).astype(np.float32)
    cosf = np.empty((128, T), dtype=np.float32)
    sinf = np.empty((128, T), dtype=np.float32)
    for dd in range(128):
        i = (dd % D) // 2
        cosf[dd] = cos[:, i]
        sinf[dd] = sin[:, i]
    masks = np.zeros((5, 128, 512), dtype=np.float32)
    kp = np.arange(128)[:, None]
    qf = np.arange(512)[None, :]
    for sidx, s in enumerate((0, 128, 256, 384, 512)):
        masks[sidx] = ((kp + s) <= (qf + 64)).astype(np.float32)
    masks = masks.transpose(1, 0, 2).reshape(128, 5 * 512)
    return dict(
        cst_ident=ident, cst_swap=swap.astype(BF), cst_ones=ones,
        cst_cos=cosf.astype(BF), cst_sin=sinf.astype(BF),
        cst_mask=masks.astype(BF),
    )


_CACHE = {}


def kernel(**inputs):
    if "nc" not in _CACHE:
        _CACHE["nc"] = build_program()
    nc = _CACHE["nc"]
    cst = make_constants()
    x = np.ascontiguousarray(np.asarray(inputs["L0_attn_x"], dtype=np.float32))
    f32 = lambda a: np.ascontiguousarray(np.asarray(a, dtype=np.float32))
    shared = dict(
        wq=f32(inputs["L0_attn_Wq"]), wk=f32(inputs["L0_attn_Wk"]),
        wv=f32(inputs["L0_attn_Wv"]), wp=f32(inputs["L0_attn_Wp"]),
        bq2=f32(np.asarray(inputs["L0_attn_bq"]).reshape(NCC, 128).T),
        bk2=f32(np.asarray(inputs["L0_attn_bk"]).reshape(NCC, 128).T),
        bv2=f32(np.asarray(inputs["L0_attn_bv"]).reshape(1, C)),
        w1=f32(inputs["L0_ffn_w1_w"]), w2=f32(inputs["L0_ffn_w2_w"]),
        h1=f32(inputs["L0_ffn_h1_w"]), h2=f32(inputs["L0_ffn_h2_w"]),
        w3=f32(inputs["L0_ffn_w3_w"]),
        w1s2=f32(np.asarray(inputs["L0_ffn_w1_s"]).reshape(NHC, 128).T),
        w2s2=f32(np.asarray(inputs["L0_ffn_w2_s"]).reshape(NHC, 128).T),
        h1s2=f32(np.asarray(inputs["L0_ffn_h1_s"]).reshape(NHC, 128).T),
        h2s2=f32(np.asarray(inputs["L0_ffn_h2_s"]).reshape(NHC, 128).T),
        w3s2=f32(np.asarray(inputs["L0_ffn_w3_s"]).reshape(NCC, 128).T),
        **{k: np.ascontiguousarray(v) for k, v in cst.items()},
    )
    in_maps = [dict(shared, x=np.ascontiguousarray(x[b])) for b in range(8)]
    res = run_bass_kernel_spmd(nc, in_maps, core_ids=list(range(8)))
    return np.stack([r["out"] for r in res.results], axis=0)


# version 21
# speedup vs baseline: 1.0603x; 1.0603x over previous
import contextlib
import numpy as np
import ml_dtypes
import concourse.bass as bass
import concourse.mybir as mybir
from concourse.bass_utils import run_bass_kernel_spmd

F32 = mybir.dt.float32
BF16 = mybir.dt.bfloat16
AF = mybir.ActivationFunctionType
ALU = mybir.AluOpType
AX = mybir.AxisListType
BF = ml_dtypes.bfloat16

T = 4096
C = 1024
D = 64
DH = 4096
NCC = C // 128
NHC = DH // 128
NT = T // 128
NQ = T // 512
NPAIR = 8
EPS = 1e-5
EXP_SHIFT = -2.0

ENGINES = ("sync", "gpsimd", "scalar", "vector", "tensor")
BUILD_PHASES = "ABDE"
DEBUG_TAPS = ()


class Sem:
    __slots__ = ("h", "count", "only_engine", "is_dma", "name")

    def __init__(self, nc, name, is_dma=False):
        self.h = nc.alloc_semaphore(name)
        self.name = name
        self.count = 0
        self.only_engine = None
        self.is_dma = is_dma


class Prog:
    def __init__(self, nc):
        self.nc = nc
        self.items = {e: [] for e in ENGINES}
        self.barrier = ()

    def emit(self, eng, fn, waits=(), incs=(), fwaits=()):
        ws = []
        for s, v in tuple(fwaits):
            if v <= 0:
                continue
            assert v <= s.count, f"wait beyond emitted incs on {s.name}: {v} > {s.count}"
            ws.append((s, v))
        for s, v in tuple(waits) + tuple(self.barrier):
            if v <= 0:
                continue
            assert v <= s.count, f"wait beyond emitted incs on {s.name}: {v} > {s.count}"
            if (not s.is_dma) and s.only_engine == eng:
                continue
            ws.append((s, v))
        for s, n in incs:
            s.count += n
            if s.only_engine is None:
                s.only_engine = eng
            elif s.only_engine != eng:
                s.only_engine = False
        self.items[eng].append((tuple(ws), fn, tuple(incs)))

    def dma(self, q, out, in_, waits=(), sem=None, **kw):
        s = sem if sem is not None else self.dma_sems[q]
        self.emit(q, lambda e, o=out, i=in_, k=kw: e.dma_start(out=o, in_=i, **k),
                  waits, ((s, 16),))
        return s.count

    def set_barrier(self, full=True):
        sems = self.all_sems if full else self.compute_sems
        self.barrier = tuple((s, s.count) for s in sems if s.count > 0)

    def finalize(self, block):
        for eng in ENGINES:
            items = self.items[eng]

            def mk(eng_items):
                def run(e):
                    waited = {}
                    for ws, fn, incs in eng_items:
                        for s, v in ws:
                            if waited.get(id(s), -1) >= v:
                                continue
                            e.wait_ge(s.h, v)
                            waited[id(s)] = v
                        if fn is None:
                            continue
                        ins = fn(e)
                        for s, n in incs:
                            ins = ins.then_inc(s.h, n)
                return run

            getattr(block, eng)(mk(items))


def ts(i, n):
    return slice(i * n, (i + 1) * n)


def build_program():
    nc = bass.Bass("TRN2", target_bir_lowering=False)

    x_in_d = nc.dram_tensor("x", [T, C], F32, kind="ExternalInput").ap()
    wq_d = nc.dram_tensor("wq", [C, C], F32, kind="ExternalInput").ap()
    wk_d = nc.dram_tensor("wk", [C, C], F32, kind="ExternalInput").ap()
    wv_d = nc.dram_tensor("wv", [C, C], F32, kind="ExternalInput").ap()
    wp_d = nc.dram_tensor("wp", [C, C], F32, kind="ExternalInput").ap()
    bq_d = nc.dram_tensor("bq2", [128, NCC], F32, kind="ExternalInput").ap()
    bk_d = nc.dram_tensor("bk2", [128, NCC], F32, kind="ExternalInput").ap()
    bv_d = nc.dram_tensor("bv2", [1, C], F32, kind="ExternalInput").ap()
    w1_d = nc.dram_tensor("w1", [DH, C], F32, kind="ExternalInput").ap()
    w2_d = nc.dram_tensor("w2", [DH, C], F32, kind="ExternalInput").ap()
    h1_d = nc.dram_tensor("h1", [DH, DH], F32, kind="ExternalInput").ap()
    h2_d = nc.dram_tensor("h2", [DH, DH], F32, kind="ExternalInput").ap()
    w3_d = nc.dram_tensor("w3", [C, DH], F32, kind="ExternalInput").ap()
    w1s_d = nc.dram_tensor("w1s2", [128, NHC], F32, kind="ExternalInput").ap()
    w2s_d = nc.dram_tensor("w2s2", [128, NHC], F32, kind="ExternalInput").ap()
    h1s_d = nc.dram_tensor("h1s2", [128, NHC], F32, kind="ExternalInput").ap()
    h2s_d = nc.dram_tensor("h2s2", [128, NHC], F32, kind="ExternalInput").ap()
    w3s_d = nc.dram_tensor("w3s2", [128, NCC], F32, kind="ExternalInput").ap()
    ident_d = nc.dram_tensor("cst_ident", [128, 128], BF16, kind="ExternalInput").ap()
    swap_d = nc.dram_tensor("cst_swap", [128, 128], BF16, kind="ExternalInput").ap()
    ones_d = nc.dram_tensor("cst_ones", [1, 128], BF16, kind="ExternalInput").ap()
    cos_d = nc.dram_tensor("cst_cos", [128, T], BF16, kind="ExternalInput").ap()
    sin_d = nc.dram_tensor("cst_sin", [128, T], BF16, kind="ExternalInput").ap()
    mask_d = nc.dram_tensor("cst_mask", [128, 5 * 512], BF16, kind="ExternalInput").ap()

    out_d = nc.dram_tensor("out", [T, C], F32, kind="ExternalOutput").ap()

    def scratch(name, shape, dt):
        kind = "ExternalOutput" if name in DEBUG_TAPS else "Internal"
        return nc.dram_tensor(name, shape, dt, kind=kind).ap()

    w1tt_d = nc.dram_tensor("w1tt", [NHC, 128, NCC, 128], BF16, kind="Internal").ap()
    w2tt_d = nc.dram_tensor("w2tt", [NHC, 128, NCC, 128], BF16, kind="Internal").ap()
    h1tt_d = nc.dram_tensor("h1tt", [NHC, 128, NHC, 128], BF16, kind="Internal").ap()
    h2tt_d = nc.dram_tensor("h2tt", [NHC, 128, NHC, 128], BF16, kind="Internal").ap()
    w3tt_d = nc.dram_tensor("w3tt", [NCC, 128, NHC, 128], BF16, kind="Internal").ap()
    attn_sp_d = scratch("attn_sp", [NPAIR, 128, T], BF16)
    xn2_sp_d = scratch("xn2_sp", [NCC, 128, T], BF16)
    x1_sp_d = scratch("x1_sp", [T, C], F32)

    A = nc.alloc_sbuf_tensor
    ident = A("ident", [128, 128], BF16)
    ones_r = A("ones_r", [1, 128], BF16)
    bq_s = A("bq_s", [128, NCC], F32)
    bk_s = A("bk_s", [128, NCC], F32)
    bv_r = A("bv_r", [1, C], F32)
    bv_rb = A("bv_rb", [1, C], BF16)
    w1s = A("w1s", [128, NHC], F32)
    w2s = A("w2s", [128, NHC], F32)
    h1s = A("h1s", [128, NHC], F32)
    h2s = A("h2s", [128, NHC], F32)
    w3s = A("w3s", [128, NCC], F32)
    eps_t = A("eps_t", [128, 1], F32)
    m2_t = A("m2_t", [128, 1], F32)
    stat = A("stat", [128, 16], F32)
    sq_scr = A("sq_scr", [128, C], BF16)

    P = nc.alloc_psum_tensor
    TRp = P("TRp", [128, 1024], BF16)
    PS1 = P("PS1", [128, 1024], F32)
    PS2 = P("PS2", [128, 1024], F32)
    PA = P("PA", [128, 1024], F32)
    SP = P("SP", [128, 512], F32)
    PBANKS = (PS1[:, 0:512], PS1[:, 512:1024], PS2[:, 0:512], PS2[:, 512:1024],
              PA[:, 0:512], PA[:, 512:1024])

    p = Prog(nc)
    sd_sy = Sem(nc, "sd_sy", True)
    sd_sc = Sem(nc, "sd_sc", True)
    sd_gp = Sem(nc, "sd_gp", True)
    s_te = Sem(nc, "s_te")
    s_ve = Sem(nc, "s_ve")
    s_se = Sem(nc, "s_se")
    s_ge = Sem(nc, "s_ge")
    SX = Sem(nc, "SX", True)
    SQK = Sem(nc, "SQK", True)
    SSP = Sem(nc, "SSP", True)
    SLB = Sem(nc, "SLB", True)
    SX1 = Sem(nc, "SX1", True)
    SXL = Sem(nc, "SXL", True)
    SOUT = Sem(nc, "SOUT", True)
    SXN = Sem(nc, "SXN", True)
    SX2 = Sem(nc, "SX2", True)
    SSL = (Sem(nc, "SSLA", True), Sem(nc, "SSLB", True))
    SPL = tuple(Sem(nc, f"SPL{i}", True) for i in range(3))
    SPS = tuple(Sem(nc, f"SPS{i}", True) for i in range(3))
    SP2 = Sem(nc, "SP2", True)
    p.dma_sems = {"sync": sd_sy, "scalar": sd_sc, "gpsimd": sd_gp}
    p.all_sems = (sd_sy, sd_sc, sd_gp, s_te, s_ve, s_se, s_ge, SX, SQK, SSP, SLB,
                  SX1, SXL, SOUT, SXN, SX2, SSL[0], SSL[1], SP2) + SPL + SPS
    p.compute_sems = (s_te, s_ve, s_se, s_ge)

    es_ab = contextlib.ExitStack()
    es_d = contextlib.ExitStack()
    es_e = contextlib.ExitStack()

    def sb(es, name, shape, dt):
        return es.enter_context(nc.sbuf_tensor(name, shape, dt))

    swap = sb(es_ab, "swap", [128, 128], BF16)
    cosf = sb(es_ab, "cosf", [128, T], BF16)
    sinf = sb(es_ab, "sinf", [128, T], BF16)
    masks = sb(es_ab, "masks", [128, 5 * 512], BF16)
    bv_bc = sb(es_ab, "bv_bc", [128, C], BF16)
    x_ld = sb(es_ab, "x_ld", [128, 2, C], F32)
    xn_tok = sb(es_ab, "xn_tok", [128, 2, C], BF16)
    xn1 = sb(es_ab, "xn1", [128, NCC, T], BF16)
    wq_sl = sb(es_ab, "wq_sl", [128, 2, NCC, 128], BF16)
    wk_sl = sb(es_ab, "wk_sl", [128, 2, NCC, 128], BF16)
    wv_sl = sb(es_ab, "wv_sl", [128, 2, NCC, 128], BF16)
    q_rot = sb(es_ab, "q_rot", [128, T], BF16)
    k_rot = sb(es_ab, "k_rot", [128, T], BF16)
    qk_raw = sb(es_ab, "qk_raw", [128, 2, 512], BF16)
    tmp1 = sb(es_ab, "tmp1", [128, 512], BF16)
    tmp2 = sb(es_ab, "tmp2", [128, 512], BF16)
    v_pair = sb(es_ab, "v_pair", [128, NT, 160], BF16)
    expP = sb(es_ab, "expP", [128, 2, 1024], BF16)
    attn_un = sb(es_ab, "attn_un", [128, 512], BF16)
    recip_b = sb(es_ab, "recip_b", [1, 1024], BF16)
    recip_f = sb(es_ab, "recip_f", [1, 1024], F32)
    attn_nrm = sb(es_ab, "attn_nrm", [128, T], BF16)
    qkv_f32 = sb(es_ab, "qkv_f32", [128, NCC, 128], F32)
    pp_f32 = sb(es_ab, "pp_f32", [128, 3, 1024], F32)
    pp_bf = sb(es_ab, "pp_bf", [128, 3, 1024], BF16)
    pp_st = sb(es_ab, "pp_st", [128, 2, 1024], BF16)

    for dst, src in ((ident, ident_d), (swap, swap_d), (ones_r, ones_d),
                     (cosf, cos_d), (sinf, sin_d), (masks, mask_d),
                     (bq_s, bq_d), (bk_s, bk_d), (bv_r, bv_d),
                     (w1s, w1s_d), (w2s, w2s_d), (h1s, h1s_d), (h2s, h2s_d),
                     (w3s, w3s_d)):
        v_pre = p.dma("scalar", dst[:], src)
    p.emit("vector", lambda e: e.memset(eps_t[:], EPS), incs=((s_ve, 1),))
    p.emit("vector", lambda e: e.memset(m2_t[:], EXP_SHIFT), incs=((s_ve, 1),))
    p.emit("vector", lambda e: e.tensor_copy(bv_rb[:], bv_r[:]),
           waits=((sd_sc, v_pre),), incs=((s_ve, 1),))
    v_bvrb = s_ve.count
    p.emit("tensor", lambda e: e.matmul(SP[:, :], ones_r[:, :], bv_rb[:, 0:512],
                                        start=True, stop=True),
           waits=((s_ve, v_bvrb), (sd_sc, v_pre)))
    p.emit("tensor", lambda e: e.matmul(PA[:, 0:512], ones_r[:, :], bv_rb[:, 512:1024],
                                        start=True, stop=True), incs=((s_te, 1),))
    v_bvmm = s_te.count
    p.emit("vector", lambda e: e.tensor_copy(bv_bc[:, 0:512], SP[:, :]),
           waits=((s_te, v_bvmm),))
    p.emit("vector", lambda e: e.tensor_copy(bv_bc[:, 512:1024], PA[:, 0:512]),
           incs=((s_ve, 1),))
    v_bvbc = s_ve.count
    p.emit("vector", lambda e: e.memset(v_pair[:, :, 64:65], 1.0), incs=((s_ve, 1),))
    p.emit("vector", lambda e: e.memset(v_pair[:, :, 144:145], 1.0), incs=((s_ve, 1),))
    v_ones_v = s_ve.count

    qkv_state = {"i": 0, "cast_v": {}}

    def load_qkv_slice(mat_d, dst_sl, j, war_waits=()):
        i = qkv_state["i"]
        qkv_state["i"] = i + 1
        wl = []
        if i >= 1:
            wl.append((s_ve, qkv_state["cast_v"][i - 1]))
        v_ld = p.dma("scalar", qkv_f32[:],
                     mat_d[:, ts(j, 128)].rearrange("(cc p) o -> p cc o", p=128),
                     waits=wl, sem=SQK)
        p.emit("vector",
               lambda e, dst_sl=dst_sl, j=j: e.tensor_copy(
                   dst_sl[:, j % 2, :, :], qkv_f32[:]),
               waits=[(SQK, v_ld)] + list(war_waits), incs=((s_ve, 1),))
        qkv_state["cast_v"][i] = s_ve.count
        return s_ve.count

    v_wq, v_wk, v_wv = {}, {}, {}
    for j in range(2):
        v_wq[j] = load_qkv_slice(wq_d, wq_sl, j)
        v_wk[j] = load_qkv_slice(wk_d, wk_sl, j)
        v_wv[j] = load_qkv_slice(wv_d, wv_sl, j)

    MATS = (("w1", w1_d, w1tt_d, DH, C), ("w2", w2_d, w2tt_d, DH, C),
            ("h1", h1_d, h1tt_d, DH, DH), ("h2", h2_d, h2tt_d, DH, DH),
            ("w3", w3_d, w3tt_d, C, DH))
    pp_units = []
    pp_last = {}
    for name, W_dd, wtt_dd, odim, cdim in MATS:
        for oc in range(odim // 128):
            for h0 in range(0, cdim, 1024):
                pp_units.append((name, W_dd, wtt_dd, oc, h0, min(1024, cdim - h0)))
                pp_last[name] = len(pp_units) - 1
    pp_state = {"ld": 0, "c1": 0, "tr": 0, "ld_v": {}, "cast_v": {}, "tr_v": {},
                "ev_v": {}, "st_v": {}}
    wt_done_v = {}
    trp_guard = [(s_ve, 0)]

    def pp_load_step():
        i = pp_state["ld"]
        if i < len(pp_units) and (i < 3 or (i - 3) in pp_state["cast_v"]):
            name, W_dd, wtt_dd, oc, h0, hw = pp_units[i]
            wl = []
            if i >= 3:
                wl.append((s_ge, pp_state["cast_v"][i - 3]))
            pp_state["ld_v"][i] = p.dma("sync", pp_f32[:, i % 3, 0:hw],
                                        W_dd[ts(oc, 128), h0:h0 + hw], waits=wl,
                                        sem=SPL[i % 3])
            pp_state["ld"] = i + 1
        c = pp_state["c1"]
        if (c < pp_state["ld"]
                and (c < pp_state["ld"] - 2 or pp_state["ld"] == len(pp_units))
                and (c < 3 or (c - 3) in pp_state["tr_v"])):
            name, W_dd, wtt_dd, oc, h0, hw = pp_units[c]
            d = c % 3
            wl = [(SPL[d], pp_state["ld_v"][c])]
            if c >= 3:
                wl.append((s_te, pp_state["tr_v"][c - 3]))
            p.emit("gpsimd",
                   lambda e, d=d, hw=hw: e.tensor_copy(pp_bf[:, d, 0:hw],
                                                       pp_f32[:, d, 0:hw]),
                   waits=wl, incs=((s_ge, 1),))
            pp_state["cast_v"][c] = s_ge.count
            pp_state["c1"] = c + 1
        return pp_state["c1"] < len(pp_units)

    def pp_tr_step():
        t = pp_state["tr"]
        if t >= min(pp_state["c1"], len(pp_units)):
            return pp_state["tr"] < len(pp_units)
        name, W_dd, wtt_dd, oc, h0, hw = pp_units[t]
        d = t % 3
        e2 = t % 2
        nblk = hw // 128
        for k in range(nblk):
            w2_ = [(s_ge, pp_state["cast_v"][t])]
            if k == 0:
                w2_.append(trp_guard[0])
            p.emit("tensor",
                   lambda e, d=d, k=k: e.transpose(
                       TRp[:, ts(k, 128)], pp_bf[:, d, ts(k, 128)], ident[:]),
                   waits=w2_, incs=((s_te, 1),) if k == nblk - 1 else ())
        pp_state["tr_v"][t] = s_te.count
        wl = [(s_te, pp_state["tr_v"][t])]
        if t >= 2:
            wl.append((SPS[e2], pp_state["st_v"][t - 2]))
        p.emit("vector",
               lambda e, e2=e2, nblk=nblk: e.tensor_copy(
                   pp_st[:, e2, 0:nblk * 128], TRp[:, 0:nblk * 128]),
               waits=wl, incs=((s_ve, 1),))
        trp_guard[0] = (s_ve, s_ve.count)
        b0 = h0 // 128
        pp_state["st_v"][t] = p.dma("sync", wtt_dd[oc, :, b0:b0 + nblk, :],
                                    pp_st[:, e2, 0:hw].rearrange("p (a b) -> p a b", b=128),
                                    waits=[(s_ve, s_ve.count)], sem=SPS[e2])
        if pp_last[name] == t:
            wt_done_v[name] = (SPS[0].count, SPS[1].count)
        pp_state["tr"] = t + 1
        return pp_state["tr"] < len(pp_units)

    def emit_pp_unit():
        return pp_load_step()

    def pp2_step():
        pp_load_step()
        return pp_tr_step()

    def ln(src_ap, dst_ap, sb_base, dep_waits, war_waits):
        su = stat[:, sb_base + 0:sb_base + 1]
        nm = stat[:, sb_base + 1:sb_base + 2]
        sq = stat[:, sb_base + 2:sb_base + 3]
        sd_ = stat[:, sb_base + 3:sb_base + 4]
        rs = stat[:, sb_base + 4:sb_base + 5]
        dep_waits = list(dep_waits)
        p.emit("vector", lambda e: e.tensor_reduce(su, src_ap, AX.X, ALU.add),
               waits=dep_waits, incs=((s_ve, 1),))
        v_su = s_ve.count
        p.emit("vector", lambda e: e.tensor_scalar(nm, su, -1.0 / C, None, ALU.mult),
               waits=[(s_se, s_se.count)], fwaits=[(s_ve, v_su)], incs=((s_ve, 1),))
        v_nm = s_ve.count
        p.emit("scalar", lambda e: e.activation(sq_scr[:], src_ap, AF.Square,
                                                bias=nm, accum_out=sq),
               waits=dep_waits + [(s_ve, v_nm)], incs=((s_se, 1),))
        v_sq = s_se.count
        p.emit("scalar", lambda e: e.activation(sd_, sq, AF.Sqrt, bias=eps_t[:],
                                                scale=1.0 / C),
               waits=[(s_ve, 2)], fwaits=[(s_se, v_sq)], incs=((s_se, 1),))
        v_sd = s_se.count
        p.emit("vector", lambda e: e.reciprocal(rs, sd_), waits=[(s_se, v_sd)],
               incs=((s_ve, 1),))
        v_rs = s_ve.count
        p.emit("vector", lambda e: e.tensor_scalar(dst_ap, src_ap, nm, rs,
                                                   ALU.add, ALU.mult),
               waits=list(war_waits), fwaits=[(s_ve, v_rs)], incs=((s_ve, 1),))
        return s_ve.count

    xld_v = {}
    xnA_copy_v = {}
    normA_v = {}
    sqA_v = {}
    if "A" in BUILD_PHASES:
        def a_ln(i):
            w = []
            if i >= 2:
                w += [(s_ve, normA_v[i - 2]), (s_se, sqA_v[i - 2])]
            xld_v[i] = p.dma("scalar", x_ld[:, i % 2, :], x_in_d[ts(i, 128), :],
                             waits=w, sem=SX)
            war = [(s_te, trA_v[i - 2])] if i >= 2 else []
            normA_v[i] = ln(x_ld[:, i % 2, :], xn_tok[:, i % 2, :], 8 * (i % 2),
                            [(SX, xld_v[i])], war)
            sqA_v[i] = s_se.count

        def a_tr(i):
            for cc in range(NCC):
                w2 = [(s_ve, normA_v[i])]
                if cc == 0:
                    w2 += [(sd_sc, v_pre), trp_guard[0]]
                p.emit("tensor",
                       lambda e, cc=cc, i=i: e.transpose(TRp[:, ts(cc, 128)],
                                                         xn_tok[:, i % 2, ts(cc, 128)],
                                                         ident[:]),
                       waits=w2, incs=((s_te, 1),) if cc == NCC - 1 else ())
            trA_v[i] = s_te.count
            p.emit("vector",
                   lambda e, i=i: e.tensor_copy(
                       xn1[:, :, ts(i, 128)],
                       TRp[:].rearrange("p (a b) -> p a b", b=128)),
                   waits=[(s_te, trA_v[i])], incs=((s_ve, 1),))
            trp_guard[0] = (s_ve, s_ve.count)
            xnA_copy_v[i] = s_ve.count

        trA_v = {}
        for i in range(NT):
            a_ln(i)
            if i >= 1:
                a_tr(i - 1)
                pp2_step()
        a_tr(NT - 1)
        pp2_step()
        if "xn1" in DEBUG_TAPS:
            tap = nc.dram_tensor("tap_xn1", [128, NCC * T], BF16, kind="ExternalOutput").ap()
            p.dma("scalar", tap, xn1[:].rearrange("p a b -> p (a b)"),
                  waits=((s_ve, s_ve.count),))

    spill_v = {}
    if "B" in BUILD_PHASES:
        b1_mm_ring = {}
        b2_mm_ring = {}
        v_tail = {"cp": 0, "ln": 0, "fm": 0}
        for j in range(NPAIR):
            jr = j % 2
            if j + 1 < NPAIR and j + 1 >= 2:
                jp = j + 1
                v_wq[jp] = load_qkv_slice(wq_d, wq_sl, jp, [(s_te, b1_mm_ring[jp % 2])])
                v_wk[jp] = load_qkv_slice(wk_d, wk_sl, jp, [(s_te, b1_mm_ring[jp % 2])])
                v_wv[jp] = load_qkv_slice(wv_d, wv_sl, jp, [(s_te, b2_mm_ring[jp % 2])])
            vmm_v = {}
            vevac_v = {}

            def b2_mm(kc):
                pv = SP[:, 0:128] if kc % 2 == 0 else PA[:, 0:128]
                for cc in range(NCC):
                    w0 = []
                    if cc == 0:
                        w0 = [(s_ve, xnA_copy_v[kc]), (s_ve, v_wv[j])]
                        if kc >= 2:
                            w0.append((s_ve, vevac_v[kc - 2]))
                        else:
                            w0 += [(s_ve, s_ve.count), (s_se, s_se.count)]
                    p.emit("tensor",
                           lambda e, kc=kc, cc=cc, jr=jr, pv=pv: e.matmul(
                               pv, xn1[:, cc, ts(kc, 128)], wv_sl[:, jr, cc, :],
                               start=(cc == 0), stop=(cc == NCC - 1)),
                           waits=w0, incs=((s_te, 1),) if cc == NCC - 1 else ())
                vmm_v[kc] = s_te.count

            def b2_evac(kc):
                pv = SP[:, 0:128] if kc % 2 == 0 else PA[:, 0:128]
                for hh in range(2):
                    p.emit("vector",
                           lambda e, kc=kc, hh=hh, j=j, pv=pv: e.tensor_tensor(
                               v_pair[:, kc, 80 * hh:80 * hh + 64],
                               pv[:, 64 * hh:64 * hh + 64],
                               bv_bc[:, j * 128 + 64 * hh:j * 128 + 64 * hh + 64],
                               ALU.add),
                           waits=[(s_te, vmm_v[kc]), (s_ve, v_bvbc), (s_ve, v_ones_v)],
                           incs=((s_ve, 1),) if hh == 1 else ())
                vevac_v[kc] = s_ve.count

            b2n = [0, 0]

            def b2_tick():
                if b2n[0] < NT:
                    b2_mm(b2n[0])
                    b2n[0] += 1
                    if b2n[1] < b2n[0] - 1:
                        b2_evac(b2n[1])
                        b2n[1] += 1
                elif b2n[1] < NT:
                    b2_evac(b2n[1])
                    b2n[1] += 1

            units = [(tt, qk) for tt in range(NQ) for qk in range(2)]
            evac_v = {}
            swap_v = {}
            rope_v = {}

            def b1_proj(u):
                tt, qk = units[u]
                wmat = wq_sl if qk == 0 else wk_sl
                v_w = (v_wq if qk == 0 else v_wk)[j]
                w_xn = (s_ve, xnA_copy_v[4 * tt + 3])
                for cc in range(NCC):
                    w0 = []
                    if cc == 0:
                        w0 = [w_xn, (s_ve, v_w)]
                        if u >= 2:
                            w0.append((s_se, evac_v[u - 2]))
                        else:
                            w0.append((s_se, s_se.count))
                    p.emit("tensor",
                           lambda e, cc=cc, qk=qk, wmat=wmat, tt=tt, jr=jr: e.matmul(
                               PS1[:, ts(qk, 512)], wmat[:, jr, cc, :],
                               xn1[:, cc, ts(tt, 512)],
                               start=(cc == 0), stop=(cc == NCC - 1)),
                           waits=w0, incs=((s_te, 1),) if cc == NCC - 1 else ())
                v_mm = s_te.count
                bias_s = bq_s if qk == 0 else bk_s
                p.emit("scalar",
                       lambda e, qk=qk, j=j, bias_s=bias_s: e.activation(
                           qk_raw[:, qk, :], PS1[:, ts(qk, 512)], AF.Identity,
                           bias=bias_s[:, j:j + 1]),
                       waits=[(s_te, v_mm), (sd_sc, v_pre)] +
                             ([(s_ve, rope_v[u - 2])] if u >= 2 else []),
                       incs=((s_se, 1),))
                evac_v[u] = s_se.count

            def b1_rope(u):
                tt, qk = units[u]
                w0 = [(s_se, evac_v[u]), (sd_sc, v_pre)]
                if u >= 2:
                    w0.append((s_ve, rope_v[u - 2]))
                p.emit("tensor",
                       lambda e, qk=qk: e.matmul(
                           PS2[:, ts(qk, 512)], swap[:], qk_raw[:, qk, :],
                           start=True, stop=True),
                       waits=w0, incs=((s_te, 1),))
                v_sw = s_te.count
                swap_v[u] = v_sw
                p.emit("vector",
                       lambda e, qk=qk, tt=tt: e.scalar_tensor_tensor(
                           tmp2[:], PS2[:, ts(qk, 512)], 1.0, sinf[:, ts(tt, 512)],
                           ALU.mult, ALU.mult),
                       waits=[(s_te, v_sw), (sd_sc, v_pre)])
                p.emit("vector",
                       lambda e, qk=qk, tt=tt: e.tensor_tensor(
                           tmp1[:], qk_raw[:, qk, :], cosf[:, ts(tt, 512)], ALU.mult),
                       waits=[(s_se, evac_v[u])])
                rot = q_rot if qk == 0 else k_rot
                p.emit("vector",
                       lambda e, tt=tt, rot=rot: e.tensor_tensor(
                           rot[:, ts(tt, 512)], tmp1[:], tmp2[:], ALU.add),
                       incs=((s_ve, 1),))
                rope_v[u] = s_ve.count

            for u in range(len(units)):
                b1_proj(u)
                if u >= 1:
                    b1_rope(u - 1)
                b2_tick()
                b2_tick()
                pp2_step()
            b1_rope(len(units) - 1)
            v_rot_done = s_ve.count
            b1_mm_ring[jr] = s_te.count
            while b2n[1] < NT:
                b2_tick()
            v_vdone = s_ve.count
            b2_mm_ring[jr] = vmm_v[NT - 1]
            exp_bank_v = {0: s_se.count, 1: s_se.count}
            for tt in range(NQ):
                kc_hi = min(4 * tt + 4, NT - 1)
                n_kc = kc_hi + 1
                pend = None

                def b3_scores_exp(kk):
                    Pb = kk % 2
                    off = max(0, 128 * kk - 512 * tt - 64)
                    psx = PS1 if Pb == 0 else PS2
                    p.emit("tensor",
                           lambda e, kk=kk, tt=tt, psx=psx, off=off: e.matmul(
                               psx[:, off:512], k_rot[0:64, ts(kk, 128)],
                               q_rot[0:64, tt * 512 + off:(tt + 1) * 512],
                               start=True, stop=True),
                           waits=[(s_ve, v_rot_done), (s_se, exp_bank_v[Pb])])
                    p.emit("tensor",
                           lambda e, kk=kk, tt=tt, psx=psx, off=off: e.matmul(
                               psx[:, 512 + off:1024], k_rot[64:128, ts(kk, 128)],
                               q_rot[64:128, tt * 512 + off:(tt + 1) * 512],
                               start=True, stop=True, tile_position=(64, 0)),
                           incs=((s_te, 1),))
                    v_sc = s_te.count
                    p.emit("scalar",
                           lambda e, Pb=Pb, psx=psx, off=off: e.activation(
                               expP[:, Pb, :].rearrange("p (h q) -> p h q", q=512)[:, :, off:],
                               psx[:].rearrange("p (h q) -> p h q", q=512)[:, :, off:],
                               AF.Exp, bias=m2_t[:], scale=0.125),
                           waits=[(s_te, v_sc), (s_ve, 2)], incs=((s_se, 1),))
                    v_exp = s_se.count
                    exp_bank_v[Pb] = v_exp
                    sidx = kk - 4 * tt
                    dep_av = (s_se, v_exp)
                    if sidx >= 0:
                        for hh in range(2):
                            p.emit("vector",
                                   lambda e, Pb=Pb, sidx=sidx, hh=hh, off=off: e.tensor_tensor(
                                       expP[:, Pb, hh * 512 + off:(hh + 1) * 512],
                                       expP[:, Pb, hh * 512 + off:(hh + 1) * 512],
                                       masks[:, sidx * 512 + off:(sidx + 1) * 512], ALU.mult),
                                   waits=[(s_se, v_exp), (sd_sc, v_pre)],
                                   incs=((s_ve, 1),) if hh == 1 else ())
                        dep_av = (s_ve, s_ve.count)
                    return dep_av

                def b3_av(kk, dep_av, last):
                    Pb = kk % 2
                    off = max(0, 128 * kk - 512 * tt - 64)
                    for hh in range(2):
                        w0 = [dep_av, (s_ve, v_vdone)]
                        if kk == 0 and hh == 0:
                            w0 += [(s_ve, v_tail["cp"]), (s_se, v_tail["ln"])]
                        p.emit("tensor",
                               lambda e, kk=kk, hh=hh, Pb=Pb, last=last, off=off: e.matmul(
                                   PA[0:65, hh * 512 + off:(hh + 1) * 512],
                                   v_pair[:, kk, 80 * hh:80 * hh + 65],
                                   expP[:, Pb, hh * 512 + off:(hh + 1) * 512],
                                   start=(kk == 0), stop=last),
                               waits=w0,
                               incs=((s_te, 1),) if (last and hh == 1) else ())

                for kk in range(n_kc):
                    dep_av = b3_scores_exp(kk)
                    if pend is not None:
                        b3_av(pend[0], pend[1], False)
                    pend = (kk, dep_av)
                    if kk % 3 == 1:
                        pp_load_step()
                    if kk % 9 == 2:
                        pp_tr_step()
                b3_av(pend[0], pend[1], True)
                v_av = s_te.count
                p.emit("vector", lambda e: e.tensor_copy(attn_un[0:64, :], PA[0:64, 0:512]),
                       waits=[(s_te, v_av)])
                p.emit("vector", lambda e: e.tensor_copy(attn_un[64:128, :], PA[0:64, 512:1024]),
                       incs=((s_ve, 1),))
                v_tail["cp"] = s_ve.count
                p.emit("scalar", lambda e: e.activation(recip_f[:], PA[64:65, 0:1024],
                                                        AF.Ln),
                       waits=[(s_te, v_av)], incs=((s_se, 1),))
                v_tail["ln"] = s_se.count
                p.emit("scalar", lambda e: e.activation(recip_b[:], recip_f[:],
                                                        AF.Exp, scale=-1.0),
                       fwaits=[(s_se, v_tail["ln"])], incs=((s_se, 1),))
                v_rec = s_se.count
                p.emit("tensor", lambda e: e.matmul(SP[0:64, :], ones_r[:, 0:64],
                                                    recip_b[:, 0:512], start=True, stop=True),
                       waits=[(s_se, v_rec), (s_ve, v_tail["fm"])])
                p.emit("tensor", lambda e: e.matmul(SP[64:128, :], ones_r[:, 0:64],
                                                    recip_b[:, 512:1024], start=True, stop=True,
                                                    tile_position=(0, 64)),
                       incs=((s_te, 1),))
                v_bc = s_te.count
                w_sp = [(SSP, spill_v[j - 1])] if j >= 1 else []
                p.emit("vector",
                       lambda e, tt=tt: e.tensor_tensor(
                           attn_nrm[:, ts(tt, 512)], attn_un[:], SP[:], ALU.mult),
                       waits=[(s_te, v_bc)] + w_sp, incs=((s_ve, 1),))
                v_tail["fm"] = s_ve.count

            spill_v[j] = p.dma("scalar", attn_sp_d[j], attn_nrm[:],
                               waits=[(s_ve, s_ve.count)], sem=SSP)
            if j == 0:
                for nm_, buf_ in (("qrot0", q_rot), ("krot0", k_rot)):
                    if nm_ in DEBUG_TAPS:
                        tp = nc.dram_tensor("tap_" + nm_, [128, T], BF16,
                                            kind="ExternalOutput").ap()
                        p.dma("scalar", tp, buf_[:], waits=((s_ve, v_rot_done),))
                if "vpair0" in DEBUG_TAPS:
                    tp = nc.dram_tensor("tap_vpair0", [128, NT * 160], BF16,
                                        kind="ExternalOutput").ap()
                    p.dma("scalar", tp, v_pair[:].rearrange("p a b -> p (a b)"),
                          waits=((s_ve, v_vdone),))
        while pp2_step():
            pass

    xn2_sp_v = {}
    x1_sp_v = {}
    if "D" in BUILD_PHASES:
        es_ab.close()
        p.set_barrier(full=True)
        wp_s = sb(es_d, "wp_s", [128, NCC, C], BF16)
        attn_fm = sb(es_d, "attn_fm", [128, NCC, T], BF16)
        y_fm = sb(es_d, "y_fm", [128, 2, 512], BF16)
        y_tok = sb(es_d, "y_tok", [128, 2, 4, C], BF16)
        x_ld2 = sb(es_d, "x_ld2", [128, 2, 4, C], F32)
        x1_t = sb(es_d, "x1_t", [128, 2, C], F32)
        lnout = sb(es_d, "lnout", [128, C], F32)
        xn2_t = sb(es_d, "xn2_t", [128, C], BF16)
        xn2_st = sb(es_d, "xn2_st", [128, 2, NCC * 128], BF16)

        wp_f32 = sb(es_d, "wp_f32", [128, NCC, 512], F32)
        v_wp_cast = 0
        for hf in range(2):
            v_ld = p.dma("scalar", wp_f32[:],
                         wp_d[:, ts(hf, 512)].rearrange("(cc p) o -> p cc o", p=128),
                         waits=[(s_ve, v_wp_cast)] if hf else [])
            p.emit("vector",
                   lambda e, hf=hf: e.tensor_copy(
                       wp_s[:].rearrange("p cc (h o) -> p cc h o", h=2)[:, :, hf, :],
                       wp_f32[:]),
                   waits=[(sd_sc, v_ld)], incs=((s_ve, 1),))
            v_wp_cast = s_ve.count
        v_wp = v_wp_cast
        tap_y = None
        if "ytok" in DEBUG_TAPS:
            tap_y = nc.dram_tensor("tap_ytok", [T, C], BF16, kind="ExternalOutput").ap()
        slab_v = {}
        for cc in range(NCC):
            slab_v[cc] = p.dma("scalar", attn_fm[:, cc, :], attn_sp_d[cc],
                               waits=[(SSP, spill_v[cc])], sem=SLB)
        xld2_v = {}
        ycpy_all = {}
        addD_v = {}

        def d_loads(tt):
            for tcb in range(4):
                i = tt * 4 + tcb
                w = []
                if tt >= 2:
                    w.append((s_ve, addD_v[(tt - 2) * 4 + 3]))
                xld2_v[i] = p.dma("scalar", x_ld2[:, tt % 2, tcb, :],
                                  x_in_d[ts(i, 128), :], waits=w, sem=SX)

        def d_mms(tt):
            ymm_v = {}
            yev_v = {}
            ycpy_v = {}

            def d_mm(oc):
                pb = oc % 2
                for cc in range(NCC):
                    w0 = []
                    if cc == 0:
                        w0 = [(SLB, slab_v[NCC - 1]), (s_ve, v_wp)]
                        if oc >= 2:
                            w0.append((s_se, yev_v[oc - 2]))
                        else:
                            w0.append((s_se, s_se.count))
                    p.emit("tensor",
                           lambda e, oc=oc, cc=cc, tt=tt, pb=pb: e.matmul(
                               PS1[:, ts(pb, 512)], wp_s[:, cc, ts(oc, 128)],
                               attn_fm[:, cc, ts(tt, 512)],
                               start=(cc == 0), stop=(cc == NCC - 1)),
                           waits=w0, incs=((s_te, 1),) if cc == NCC - 1 else ())
                ymm_v[oc] = s_te.count
                p.emit("scalar",
                       lambda e, pb=pb: e.activation(y_fm[:, pb, :], PS1[:, ts(pb, 512)],
                                                     AF.Identity),
                       waits=[(s_te, ymm_v[oc])] +
                             ([(s_te, s_te.count)] if oc >= 2 else []),
                       incs=((s_se, 1),))
                yev_v[oc] = s_se.count

            def d_tr(oc):
                pb = oc % 2
                for tcb in range(4):
                    w2 = [(s_se, yev_v[oc])]
                    if tcb == 0:
                        w2 += [trp_guard[0], (s_se, s_se.count)]
                    p.emit("tensor",
                           lambda e, tcb=tcb, pb=pb: e.transpose(
                               TRp[:, ts(tcb, 128)], y_fm[:, pb, ts(tcb, 128)], ident[:]),
                           waits=w2, incs=((s_te, 1),) if tcb == 3 else ())
                v_tr = s_te.count
                p.emit("vector",
                       lambda e, oc=oc, tt=tt: e.tensor_copy(
                           y_tok[:, tt % 2, :, ts(oc, 128)],
                           TRp[:, 0:512].rearrange("p (a b) -> p a b", b=128)),
                       waits=[(s_te, v_tr), (s_se, s_se.count)], incs=((s_ve, 1),))
                trp_guard[0] = (s_ve, s_ve.count)
                ycpy_v[oc] = s_ve.count

            for oc in range(NCC):
                d_mm(oc)
                if oc >= 1:
                    d_tr(oc - 1)
                pp2_step()
            d_tr(NCC - 1)
            ycpy_all[tt] = ycpy_v[NCC - 1]
            if tap_y is not None:
                p.dma("scalar", tap_y.rearrange("(a p) c -> p a c", p=128)[:, tt * 4:tt * 4 + 4, :],
                      y_tok[:, tt % 2, :, :], waits=((s_ve, ycpy_v[NCC - 1]),))

        def d_chain(tt):
            for tcb in range(4):
                i = tt * 4 + tcb
                tb = tcb % 2
                ln(y_tok[:, tt % 2, tcb, :], lnout[:], 0,
                   [(s_ve, ycpy_all[tt])], [(s_ve, s_ve.count)])
                w_x1 = [(SX, xld2_v[tt * 4 + 3]), (s_se, s_se.count)]
                if i >= 2:
                    w_x1.append((SX1, x1_sp_v[i - 2]))
                p.emit("vector",
                       lambda e, tcb=tcb, tb=tb, tt=tt: e.tensor_tensor(
                           x1_t[:, tb, :], lnout[:], x_ld2[:, tt % 2, tcb, :], ALU.add),
                       waits=w_x1, incs=((s_ve, 1),))
                v_x1 = s_ve.count
                addD_v[i] = v_x1
                x1_sp_v[i] = p.dma("scalar", x1_sp_d[ts(i, 128), :], x1_t[:, tb, :],
                                   waits=[(s_ve, v_x1)], sem=SX1)
                ln(x1_t[:, tb, :], xn2_t[:], 8, [(s_ve, v_x1)],
                   [(s_ve, s_ve.count), (s_te, s_te.count)])
                v_n3 = s_ve.count
                for cc in range(NCC):
                    w2 = [(s_ve, v_n3)]
                    if cc == 0:
                        w2.append(trp_guard[0])
                    p.emit("tensor",
                           lambda e, cc=cc: e.transpose(TRp[:, ts(cc, 128)],
                                                        xn2_t[:, ts(cc, 128)], ident[:]),
                           waits=w2, incs=((s_te, 1),) if cc == NCC - 1 else ())
                v_tr3 = s_te.count
                w3_ = [(s_te, v_tr3)]
                if i >= 2:
                    w3_.append((SXN, xn2_sp_v[i - 2]))
                p.emit("vector",
                       lambda e, tb=tb: e.tensor_copy(
                           xn2_st[:, tb, :].rearrange("p (a b) -> p a b", b=128),
                           TRp[:].rearrange("p (a b) -> p a b", b=128)),
                       waits=w3_, incs=((s_ve, 1),))
                trp_guard[0] = (s_ve, s_ve.count)
                v_st = s_ve.count
                xn2_sp_v[i] = p.dma("sync",
                                    xn2_sp_d.rearrange("cc c t -> c cc t")[:, :, ts(i, 128)],
                                    xn2_st[:, tb, :].rearrange("p (a b) -> p a b", b=128),
                                    waits=[(s_ve, v_st)], sem=SXN)
                pp2_step()

        for tt in range(NQ):
            d_loads(tt)
            d_mms(tt)
            if tt >= 1:
                d_chain(tt - 1)
        d_chain(NQ - 1)
        while pp2_step() or pp_state["tr"] < len(pp_units):
            pass

    if "E" in BUILD_PHASES:
        es_d.close()
        p.set_barrier(full=True)
        x2sl = sb(es_e, "x2sl", [128, NCC, 512], BF16)
        wslab_t = sb(es_e, "wslab", [128, 2 * NHC * 512], BF16)
        wslab = wslab_t[:].rearrange("p (d oc cc o) -> p d oc cc o", d=2, oc=4, cc=NHC, o=128)
        hbuf = sb(es_e, "hbuf", [128, NHC, 512], BF16)
        g1buf = sb(es_e, "g1buf", [128, NHC, 512], BF16)
        tsil = sb(es_e, "tsil", [128, 2, 512], BF16)
        y2fm = sb(es_e, "y2fm", [128, NCC, 512], BF16)
        y2tok = sb(es_e, "y2tok", [128, C], BF16)
        x1_ld = sb(es_e, "x1_ld", [128, 2, C], F32)
        out_t = sb(es_e, "out_t", [128, 2, C], F32)
        lnoutE = sb(es_e, "lnoutE", [128, C], F32)

        pk = [0]
        bank_free_v = {}

        def next_bank():
            b = pk[0] % 6
            pk[0] += 1
            return b

        out_v = {}
        st1_done_prev = 0
        w3_war_prev = {0: 0, 1: 0}
        addE_v = {}
        for ti in range(NQ):
            w = [(SXN, max(xn2_sp_v[ti * 4 + k] for k in range(4)))]
            if ti >= 1:
                w.append((s_te, st1_done_prev))
            v_x2 = p.dma("sync", x2sl[:],
                         xn2_sp_d.rearrange("cc c t -> c cc t")[:, :, ts(ti, 512)],
                         waits=w, sem=SX2)
            st1_mm_v = {}
            for ob in range(8):
                d = ob % 2
                w1w = [(SPS[0], wt_done_v["w1"][0]), (SPS[1], wt_done_v["w1"][1])]
                if ob >= 2:
                    w1w.append((s_te, st1_mm_v[ob - 2]))
                elif ti >= 1:
                    w1w.append((s_te, w3_war_prev[ob]))
                slv1 = p.dma("sync", wslab[:, d, :, 0:NCC, :],
                             w1tt_d[ts(ob, 4)].rearrange("oc u cc o -> u oc cc o"),
                             waits=w1w, sem=SSL[d])
                slv2 = p.dma("sync", wslab[:, d, :, NCC:2 * NCC, :],
                             w2tt_d[ts(ob, 4)].rearrange("oc u cc o -> u oc cc o"),
                             waits=[(SPS[0], wt_done_v["w2"][0]),
                                    (SPS[1], wt_done_v["w2"][1])], sem=SSL[d])
                for oc in range(4):
                    hc = ob * 4 + oc
                    b1 = next_bank()
                    b2 = next_bank()
                    for which, bb, slv in ((0, b1, slv1), (1, b2, slv2)):
                        for cc in range(NCC):
                            w0 = []
                            if cc == 0:
                                w0 = [(SSL[d], slv), (SX2, v_x2)]
                                if bb in bank_free_v:
                                    w0.append(bank_free_v[bb])
                            p.emit("tensor",
                                   lambda e, which=which, bb=bb, cc=cc, oc=oc, d=d: e.matmul(
                                       PBANKS[bb], wslab[:, d, oc, which * NCC + cc, :],
                                       x2sl[:, cc, :],
                                       start=(cc == 0), stop=(cc == NCC - 1)),
                                   waits=w0, incs=((s_te, 1),) if cc == NCC - 1 else ())
                    v_mm = s_te.count
                    p.emit("scalar",
                           lambda e, hc=hc, b1=b1: e.activation(
                               tsil[:, hc % 2, :], PBANKS[b1], AF.Silu,
                               scale=w1s[:, hc:hc + 1]),
                           waits=[(s_te, v_mm), (s_ve, s_ve.count), (sd_sc, v_pre)],
                           incs=((s_se, 1),))
                    v_sil = s_se.count
                    bank_free_v[b1] = (s_se, v_sil)
                    p.emit("vector",
                           lambda e, hc=hc, b2=b2: e.scalar_tensor_tensor(
                               hbuf[:, hc, :], PBANKS[b2], w2s[:, hc:hc + 1],
                               tsil[:, hc % 2, :], ALU.mult, ALU.mult),
                           waits=[(s_se, v_sil)], incs=((s_ve, 1),))
                    bank_free_v[b2] = (s_ve, s_ve.count)
                st1_mm_v[ob] = s_te.count
            st1_done_next = st1_mm_v[7]
            v_h_done = (s_ve, s_ve.count)
            if "hbuf" in DEBUG_TAPS and ti == 0:
                th = nc.dram_tensor("tap_hbuf", [128, NHC * 512], BF16,
                                    kind="ExternalOutput").ap()
                p.dma("scalar", th, hbuf[:].rearrange("p a b -> p (a b)"), waits=(v_h_done,))
            for wt_name, wt_dd, src, dst, scl in (("h1", h1tt_d, hbuf, g1buf, h1s),
                                                  ("h2", h2tt_d, g1buf, hbuf, h2s)):
                slab_mm_v = {}
                for ob in range(8):
                    d = ob % 2
                    w1w = [(SPS[0], wt_done_v[wt_name][0]),
                           (SPS[1], wt_done_v[wt_name][1])]
                    if ob >= 2:
                        w1w.append((s_te, slab_mm_v[ob - 2]))
                    else:
                        w1w.append((s_te, st1_mm_v[6 + ob]))
                    slv = p.dma("sync", wslab[:, d, :, :, :],
                                wt_dd[ts(ob, 4)].rearrange("oc u cc o -> u oc cc o"),
                                waits=w1w, sem=SSL[d])
                    for oc in range(4):
                        hc = ob * 4 + oc
                        bb = next_bank()
                        for cc in range(NHC):
                            w0 = []
                            if cc == 0:
                                w0 = [(SSL[d], slv), v_h_done]
                                if bb in bank_free_v:
                                    w0.append(bank_free_v[bb])
                            p.emit("tensor",
                                   lambda e, bb=bb, cc=cc, oc=oc, d=d, src=src: e.matmul(
                                       PBANKS[bb], wslab[:, d, oc, cc, :],
                                       src[:, cc, :],
                                       start=(cc == 0), stop=(cc == NHC - 1)),
                                   waits=w0, incs=((s_te, 1),) if cc == NHC - 1 else ())
                        v_mm = s_te.count
                        p.emit("scalar",
                               lambda e, hc=hc, bb=bb, dst=dst, scl=scl: e.activation(
                                   dst[:, hc, :], PBANKS[bb], AF.Gelu,
                                   scale=scl[:, hc:hc + 1]),
                               waits=[(s_te, v_mm)], incs=((s_se, 1),))
                        bank_free_v[bb] = (s_se, s_se.count)
                    slab_mm_v[ob] = s_te.count
                v_h_done = (s_se, s_se.count)
                if "g1buf" in DEBUG_TAPS and ti == 0 and wt_name == "h1":
                    tg = nc.dram_tensor("tap_g1buf", [128, NHC * 512], BF16,
                                        kind="ExternalOutput").ap()
                    p.dma("scalar", tg, g1buf[:].rearrange("p a b -> p (a b)"), waits=(v_h_done,))
                st1_mm_v = {6: slab_mm_v[6], 7: slab_mm_v[7]}
            v_g2_done = v_h_done
            for ob2 in range(2):
                d = ob2 % 2
                slv = p.dma("sync", wslab[:, d, :, :, :],
                            w3tt_d[ts(ob2, 4)].rearrange("oc u cc o -> u oc cc o"),
                            waits=[(SPS[0], wt_done_v["w3"][0]),
                                   (SPS[1], wt_done_v["w3"][1]),
                                   (s_te, st1_mm_v[6 + ob2])], sem=SSL[d])
                for oc in range(4):
                    occ = ob2 * 4 + oc
                    bb = next_bank()
                    for cc in range(NHC):
                        w0 = []
                        if cc == 0:
                            w0 = [(SSL[d], slv), v_g2_done]
                            if bb in bank_free_v:
                                w0.append(bank_free_v[bb])
                        p.emit("tensor",
                               lambda e, bb=bb, cc=cc, oc=oc, d=d: e.matmul(
                                   PBANKS[bb], wslab[:, d, oc, cc, :],
                                   hbuf[:, cc, :],
                                   start=(cc == 0), stop=(cc == NHC - 1)),
                               waits=w0, incs=((s_te, 1),) if cc == NHC - 1 else ())
                    v_mm = s_te.count
                    p.emit("scalar",
                           lambda e, occ=occ, bb=bb: e.activation(
                               y2fm[:, occ, :], PBANKS[bb], AF.Identity,
                               scale=w3s[:, occ:occ + 1]),
                           waits=[(s_te, v_mm), (s_te, s_te.count)], incs=((s_se, 1),))
                    bank_free_v[bb] = (s_se, s_se.count)
                w3_war_prev[ob2] = s_te.count
            v_y2 = s_se.count
            st1_done_prev = st1_done_next
            for tcb in range(4):
                i = ti * 4 + tcb
                tb = tcb % 2
                w_x1 = [(SX1, x1_sp_v[i])]
                if i >= 2:
                    w_x1.append((s_ve, addE_v[i - 2]))
                v_x1ld = p.dma("scalar", x1_ld[:, tb, :], x1_sp_d[ts(i, 128), :],
                               waits=w_x1, sem=SXL)
                for cc in range(NCC):
                    w2 = [(s_se, v_y2)]
                    if cc == 0:
                        w2.append(trp_guard[0])
                    p.emit("tensor",
                           lambda e, cc=cc, tcb=tcb: e.transpose(
                               TRp[:, ts(cc, 128)], y2fm[:, cc, ts(tcb, 128)], ident[:]),
                           waits=w2, incs=((s_te, 1),) if cc == NCC - 1 else ())
                v_tr = s_te.count
                p.emit("vector", lambda e: e.tensor_copy(y2tok[:], TRp[:]),
                       waits=[(s_te, v_tr), (s_se, s_se.count)], incs=((s_ve, 1),))
                trp_guard[0] = (s_ve, s_ve.count)
                ln(y2tok[:], lnoutE[:], 0, [(s_ve, s_ve.count)], [(s_ve, s_ve.count)])
                w_add = [(SXL, v_x1ld)]
                if i >= 2:
                    w_add.append((SOUT, out_v[i - 2]))
                p.emit("vector",
                       lambda e, tb=tb: e.tensor_tensor(
                           out_t[:, tb, :], lnoutE[:], x1_ld[:, tb, :], ALU.add),
                       waits=w_add, incs=((s_ve, 1),))
                addE_v[i] = s_ve.count
                out_v[i] = p.dma("scalar", out_d[ts(i, 128), :], out_t[:, tb, :],
                                 waits=[(s_ve, addE_v[i])], sem=SOUT)
        p.emit("sync", None, waits=[(SOUT, out_v[NT - 1])])
        es_e.close()
    else:
        es_ab.close()

    with nc.Block() as block:
        p.finalize(block)
    return nc


def make_constants():
    ident = np.eye(128, dtype=BF)
    swap = np.zeros((128, 128), dtype=np.float32)
    for i in range(64):
        swap[2 * i, 2 * i + 1] = 1.0
        swap[2 * i + 1, 2 * i] = -1.0
    ones = np.ones((1, 128), dtype=BF)
    inv = 1.0 / (10000.0 ** (np.arange(0, D, 2, dtype=np.float32) / D))
    ang = np.arange(T, dtype=np.float32)[:, None] * inv[None, :]
    cos = np.cos(ang).astype(np.float32)
    sin = np.sin(ang).astype(np.float32)
    cosf = np.empty((128, T), dtype=np.float32)
    sinf = np.empty((128, T), dtype=np.float32)
    for dd in range(128):
        i = (dd % D) // 2
        cosf[dd] = cos[:, i]
        sinf[dd] = sin[:, i]
    masks = np.zeros((5, 128, 512), dtype=np.float32)
    kp = np.arange(128)[:, None]
    qf = np.arange(512)[None, :]
    for sidx, s in enumerate((0, 128, 256, 384, 512)):
        masks[sidx] = ((kp + s) <= (qf + 64)).astype(np.float32)
    masks = masks.transpose(1, 0, 2).reshape(128, 5 * 512)
    return dict(
        cst_ident=ident, cst_swap=swap.astype(BF), cst_ones=ones,
        cst_cos=cosf.astype(BF), cst_sin=sinf.astype(BF),
        cst_mask=masks.astype(BF),
    )


_CACHE = {}


def kernel(**inputs):
    if "nc" not in _CACHE:
        _CACHE["nc"] = build_program()
    nc = _CACHE["nc"]
    cst = make_constants()
    x = np.ascontiguousarray(np.asarray(inputs["L0_attn_x"], dtype=np.float32))
    f32 = lambda a: np.ascontiguousarray(np.asarray(a, dtype=np.float32))
    shared = dict(
        wq=f32(inputs["L0_attn_Wq"]), wk=f32(inputs["L0_attn_Wk"]),
        wv=f32(inputs["L0_attn_Wv"]), wp=f32(inputs["L0_attn_Wp"]),
        bq2=f32(np.asarray(inputs["L0_attn_bq"]).reshape(NCC, 128).T),
        bk2=f32(np.asarray(inputs["L0_attn_bk"]).reshape(NCC, 128).T),
        bv2=f32(np.asarray(inputs["L0_attn_bv"]).reshape(1, C)),
        w1=f32(inputs["L0_ffn_w1_w"]), w2=f32(inputs["L0_ffn_w2_w"]),
        h1=f32(inputs["L0_ffn_h1_w"]), h2=f32(inputs["L0_ffn_h2_w"]),
        w3=f32(inputs["L0_ffn_w3_w"]),
        w1s2=f32(np.asarray(inputs["L0_ffn_w1_s"]).reshape(NHC, 128).T),
        w2s2=f32(np.asarray(inputs["L0_ffn_w2_s"]).reshape(NHC, 128).T),
        h1s2=f32(np.asarray(inputs["L0_ffn_h1_s"]).reshape(NHC, 128).T),
        h2s2=f32(np.asarray(inputs["L0_ffn_h2_s"]).reshape(NHC, 128).T),
        w3s2=f32(np.asarray(inputs["L0_ffn_w3_s"]).reshape(NCC, 128).T),
        **{k: np.ascontiguousarray(v) for k, v in cst.items()},
    )
    in_maps = [dict(shared, x=np.ascontiguousarray(x[b])) for b in range(8)]
    res = run_bass_kernel_spmd(nc, in_maps, core_ids=list(range(8)))
    return np.stack([r["out"] for r in res.results], axis=0)
